# version 04
# baseline (speedup 1.0000x reference)
import numpy as np
from contextlib import ExitStack

import concourse.bass as bass
import concourse.mybir as mybir
from concourse.bass_utils import run_bass_kernel_spmd

F32 = mybir.dt.float32
BF16 = mybir.dt.bfloat16
F8 = mybir.dt.float8e4
U8 = mybir.dt.uint8
AF = mybir.ActivationFunctionType
ALU = mybir.AluOpType
AX = mybir.AxisListType

NCORES = 8
D = 1024
DC = 8
DF = 4096
FC = 32
TC = 2048
NB = 4
TB = 512
KIB = 1024

ENGINES = ("tensor", "scalar", "vector", "gpsimd", "sync")


class Planner:
    def __init__(self):
        self.ops = {e: [] for e in ENGINES}
        self.cnt = {}
        self.lastw = {e: {} for e in ENGINES}
        self.vsem = None

    def total(self, sem):
        return self.cnt.get(sem.name, 0)

    def wait(self, eng, sem, val):
        if sem is None or val is None or val <= 0:
            return
        if self.lastw[eng].get(sem.name, 0) >= val:
            return
        self.lastw[eng][sem.name] = val
        self.ops[eng].append(lambda E, s=sem, v=val: E.wait_ge(s, v))

    def op(self, eng, build, sem=None, n=1, waits=()):
        if eng == "vector" and self.vsem is not None:
            if sem is None:
                sem = self.vsem
            if sem is self.vsem:
                prev = self.cnt.get(sem.name, 0)
                if prev > 0:
                    self.wait(eng, sem, prev)
        for s, v in waits:
            self.wait(eng, s, v)
        if sem is None:
            self.ops[eng].append(build)
            return None
        self.cnt[sem.name] = self.cnt.get(sem.name, 0) + n
        tgt = self.cnt[sem.name]
        self.ops[eng].append(lambda E, b=build, s=sem, k=n: b(E).then_inc(s, k))
        return tgt


def build_nc(sublayers=3, debug_nb=NB):
    nc = bass.Bass("TRN2", target_bir_lowering=False, debug=False)

    xT = nc.dram_tensor("xT", [D, TC], F32, kind="ExternalInput").ap()
    wupT = nc.dram_tensor("wupT", [D, DF], F32, kind="ExternalInput").ap()
    wdnT = nc.dram_tensor("wdnT", [DF, D], F32, kind="ExternalInput").ap()
    wdecT = nc.dram_tensor("wdecT", [D, D], F32, kind="ExternalInput").ap()
    winpT = nc.dram_tensor("winpT", [D, D], F32, kind="ExternalInput").ap()
    wup_sl = nc.dram_tensor("wup_sl", [128, DF], F32, kind="ExternalInput").ap()
    wdn_sl = nc.dram_tensor("wdn_sl", [512, D], F32, kind="ExternalInput").ap()
    bdec = nc.dram_tensor("bdec", [128, DC], F32, kind="ExternalInput").ap()
    binp = nc.dram_tensor("binp", [128, DC], F32, kind="ExternalInput").ap()
    selm = nc.dram_tensor("selm", [128, DC, 8], F32, kind="ExternalInput").ap()
    yT = nc.dram_tensor("yT", [D, TC], F32, kind="ExternalOutput").ap()
    al_in = nc.dram_tensor("al_in", [128, 2], F32)
    al_out = nc.dram_tensor("al_out", [128, 2], F32)
    ag_in = nc.dram_tensor("ag_in", [128, 16], F32)
    ag_out = nc.dram_tensor("ag_out", [8, 128, 16], F32)
    sp_up = nc.dram_tensor("sp_up", [128, FC, KIB], F8)
    sp_dn = nc.dram_tensor("sp_dn", [128, FC, KIB], F8)

    es = ExitStack()

    def sb(name, shape, dtype):
        return es.enter_context(nc.sbuf_tensor(name, shape, dtype))

    x_sb = sb("x_sb", [128, DC, TC], F32)
    ones_col_bf = sb("ones_col_bf", [128, 1], BF16)
    ones_row_f = sb("ones_row_f", [1, 128], F32)
    half_row_f = sb("half_row_f", [1, 128], F32)
    quar_row_f = sb("quar_row_f", [1, 128], F32)
    bdec_sb = sb("bdec_sb", [128, DC], F32)
    binp_sb = sb("binp_sb", [128, DC], F32)
    selm_sb = sb("selm_sb", [128, DC, 8], F32)
    exch_sb = sb("exch_sb", [128, 16], F32)
    gath_sb = sb("gath_sb", [128, 8, 16], F32)
    cs_sb = sb("cs_sb", [128, DC, 8], F32)
    tmp88_sb = sb("tmp88_sb", [128, DC, 8], F32)
    sin_sb = sb("sin_sb", [128, DC], F32)
    cpc_pp = [sb("cpc_a", [128, DC], F32), sb("cpc_b", [128, DC], F32)]
    stc_pp = [sb("stc_a", [128, DC], F32), sb("stc_b", [128, DC], F32)]
    iflat = sb("iflat", [128, TB], F32)
    oflat = sb("oflat", [128, TB], F32)
    S_pp = [sb("S_a", [128, DC], F32), sb("S_b", [128, DC], F32)]
    tmpS = sb("tmpS", [128, DC], F32)
    al_pack = sb("al_pack", [128, 2], F32)
    acc16 = sb("acc16", [128, 16], F32)
    zeros_sb = sb("zeros_sb", [128, TB], F32)
    sc_col = sb("sc_col", [128, 1], F32)
    spc_sb = sb("spc_sb", [1, 8], F32)
    al_gath = sb("al_gath", [128, 2], F32)
    th_src = sb("th_src", [1, 2], F32)
    th_up_p = sb("th_up_p", [128, 1], F32)
    th_up_n = sb("th_up_n", [128, 1], F32)
    th_dn_p = sb("th_dn_p", [128, 1], F32)
    th_dn_n = sb("th_dn_n", [128, 1], F32)
    a_row = sb("a_row", [1, TB], F32)
    b_row = sb("b_row", [1, TB], F32)
    c_row = sb("c_row", [1, TB], F32)
    r0 = sb("r0", [1, TB], F32)
    r1 = sb("r1", [1, TB], F32)
    r2 = sb("r2", [1, TB], F32)
    r3 = sb("r3", [1, TB], F32)

    ARENA = 120 * KIB
    arena = es.enter_context(nc.sbuf_tensor("arena", [128, ARENA], U8))
    arena_base = nc.lookup_mloc(arena).addr

    def at(off_kib, name, shape, dtype):
        hnd = nc.virtual_sbuf_tensor(name, shape, dtype)
        nc._tensor(hnd.name, hnd.shape, hnd.dtype, type="SB")
        mloc = nc.lookup_mloc(hnd)
        mloc.allocated = True
        mloc.addr = arena_base + int(off_kib * KIB)
        return hnd

    wup8 = at(0, "wup8", [128, FC, KIB], F8)
    wdn8 = at(32, "wdn8", [128, FC, KIB], F8)
    h_sb = at(64, "h_sb", [128, FC, TB], BF16)
    xn_sb = at(96, "xn_sb", [128, DC, TB], BF16)
    xbf_sb = at(96, "xbf_sb", [128, DC, TB], BF16)
    castup = at(104, "castup", [128, 2, DC, 128], BF16)
    castdn = at(108, "castdn", [128, 2, DC, 128], BF16)
    t1_sb = at(112, "t1_sb", [128, TB], F32)
    csb_sb = at(114, "csb_sb", [128, TB], F32)
    hsq_sb = at(116, "hsq_sb", [128, 2, TB], BF16)
    wstage = at(108, "wstage", [128, 2, TB], F32)
    wstage4 = at(104, "wstage4", [128, 4, TB], F32)
    tneg2 = at(114, "tneg2", [128, TB], BF16)
    tpos2 = at(115, "tpos2", [128, TB], BF16)
    tneg0 = at(116, "tneg0", [128, TB], BF16)
    tpos0 = at(117, "tpos0", [128, TB], BF16)
    tneg1 = at(118, "tneg1", [128, TB], BF16)
    tpos1 = at(119, "tpos1", [128, TB], BF16)
    wdec_sb = at(0, "wdec_sb", [128, DC, KIB], BF16)
    winp_sb = at(16, "winp_sb", [128, DC, KIB], BF16)
    cp_full = at(32, "cp_full", [128, DC, TC], BF16)
    dec_sb = at(64, "dec_sb", [128, DC, TB], F32)
    inj_sb = at(80, "inj_sb", [128, DC, TB], F32)
    cp32_sb = at(96, "cp32_sb", [128, DC, TB], F32)
    xn2_sb = at(112, "xn2_sb", [128, DC, TB], BF16)
    xbf2_sb = at(112, "xbf2_sb", [128, DC, TB], BF16)

    def psum(name):
        return es.enter_context(nc.psum_tensor(name, [128, TB], F32))

    pUP = [psum("pUP0"), psum("pUP1")]
    pDN = [psum("pDN0"), psum("pDN1")]
    pS1 = psum("pS1")
    pS2 = psum("pS2")
    pBA = psum("pBA")
    pBB = psum("pBB")

    def sem(name):
        return es.enter_context(nc.semaphore(name))

    s_dq = sem("s_dq")
    s_aq = sem("s_aq")
    s_gq = sem("s_gq")
    s_x = sem("s_x")
    s_w0 = sem("s_w0")
    s_w1 = sem("s_w1")
    s_w2 = sem("s_w2")
    s_w3 = sem("s_w3")
    s_sp = sem("s_sp")
    s_t = sem("s_t")
    s_s = sem("s_s")
    s_v = sem("s_v")
    s_g = sem("s_g")
    s_cc = sem("s_cc")

    P = Planner()
    P.vsem = s_v
    DMAS = {"sync": s_dq, "scalar": s_aq, "gpsimd": s_gq}
    CSEM = {"vector": s_v, "gpsimd": s_g, "scalar": s_s, "tensor": s_t}
    ST = {}

    def spacer(eng):
        P.op(eng, lambda E: E.tensor_single_scalar(spc_sb[:, :], spc_sb[:, :], 0.0,
                                                   ALU.add))

    def dma(eng, out_ap, in_ap, waits=(), sem=None):
        return P.op(eng, lambda E, o=out_ap, i=in_ap: E.dma_start(out=o, in_=i),
                    sem if sem is not None else DMAS[eng], 16, waits)

    xr = xT.rearrange("(c p) t -> p c t", p=128)
    x_tgt = None
    for i in range(4):
        x_tgt = dma("scalar", x_sb[:, 2 * i:2 * i + 2, :], xr[:, 2 * i:2 * i + 2, :],
                    sem=s_x)
    dma("scalar", bdec_sb[:, :], bdec)
    binp_tgt = dma("scalar", binp_sb[:, :], binp)
    misc_tgt = dma("scalar", selm_sb[:, :, :], selm)
    ST["x"] = x_tgt
    ST["bias_ld"] = misc_tgt

    for t_, v_ in ((ones_col_bf, 1.0), (ones_row_f, 1.0), (half_row_f, 0.5),
                   (quar_row_f, 0.25),
                   (al_pack, 0.0), (zeros_sb, 0.0), (cpc_pp[0], 1.0),
                   (stc_pp[0], 0.0), (sc_col, 0.5 / (D * DF)), (spc_sb, 0.0)):
        consts_done = P.op("vector", lambda E, t=t_, v=v_: E.memset(t[:, :], v), s_v)

    sl_tgts = []
    red_tgts = []
    for k in range(16):
        if k < 8:
            src = wup_sl[:, TB * k:TB * (k + 1)]
        else:
            kk = k - 8
            src = wdn_sl[128 * (kk // 2):128 * (kk // 2 + 1),
                         TB * (kk % 2):TB * (kk % 2 + 1)]
        wts = [(s_v, red_tgts[k - 2])] if k >= 2 else []
        slot_sem = s_w0 if k % 2 == 0 else s_w1
        sl_tgts.append(dma("sync", wstage[:, k % 2, :], src, waits=wts, sem=slot_sem))
        red_tgts.append(P.op("vector", lambda E, k=k: E.tensor_reduce(
            acc16[:, k:k + 1], wstage[:, k % 2, :], AX.X, ALU.add,
            apply_absolute_value=True),
            s_v, waits=[(slot_sem, sl_tgts[k]), (s_v, consts_done)]))
    spacer("vector")
    r_up = P.op("vector", lambda E: E.tensor_reduce(al_pack[:, 0:1], acc16[:, 0:8],
                                                    AX.X, ALU.add), s_v,
                waits=[(s_v, red_tgts[-1])])
    alred_done = P.op("vector", lambda E: E.tensor_reduce(
        al_pack[:, 1:2], acc16[:, 8:16], AX.X, ALU.add), s_v,
        waits=[(s_v, r_up)])

    def emit_stats(sub, b, pr, with_rms, x_waits, extra_mm_waits=()):
        sl = slice(TB * b, TB * (b + 1))
        xbf = xbf2_sb if sub == 1 else xbf_sb
        cwts = list(x_waits)
        if ST.get("prev_sq_mm") is not None:
            cwts.append((s_t, ST["prev_sq_mm"]))
        cst = P.op("vector", lambda E, sl=sl, xbf=xbf: E.tensor_copy(
            xbf[:, :, :], x_sb[:, :, sl]), s_v, waits=cwts)
        mm = None
        for c in range(DC):
            mm = P.op("tensor", lambda E, c=c, xbf=xbf: E.matmul(
                pS1[0:1, :], ones_col_bf[:, :], xbf[:, c, :],
                start=(c == 0), stop=(c == DC - 1)),
                s_t if c == DC - 1 else None,
                waits=([(s_v, cst)] + list(extra_mm_waits)) if c == 0 else ())
        sum_mm = mm
        m_t = P.op("vector", lambda E: E.tensor_single_scalar(
            r0[:, :], pS1[0:1, :], 1.0 / D, ALU.mult), s_v, waits=[(s_t, sum_mm)])
        sq_t = P.op("scalar", lambda E, xbf=xbf: E.activation(
            xbf[:, :, :], xbf[:, :, :], AF.Square), s_s, waits=[(s_t, sum_mm)])
        mm = None
        for c in range(DC):
            mm = P.op("tensor", lambda E, c=c, xbf=xbf: E.matmul(
                pS2[0:1, :], ones_col_bf[:, :], xbf[:, c, :],
                start=(c == 0), stop=(c == DC - 1)),
                s_t if c == DC - 1 else None,
                waits=[(s_s, sq_t)] if c == 0 else ())
        sq_mm = mm
        ST["prev_sq_mm"] = sq_mm
        m2_t = P.op("vector", lambda E: E.tensor_tensor(
            r2[:, :], r0[:, :], r0[:, :], ALU.mult), s_v, waits=[(s_v, m_t)])
        var_t = P.op("vector", lambda E: E.scalar_tensor_tensor(
            r1[:, :], pS2[0:1, :], 1.0 / D, r2[:, :], ALU.mult, ALU.subtract), s_v,
            waits=[(s_t, sq_mm), (s_v, m2_t)])
        q_t = P.op("vector", lambda E: E.tensor_single_scalar(
            r2[:, :], r1[:, :], 1e-5, ALU.add), s_v, waits=[(s_v, var_t)])
        sq1 = P.op("scalar", lambda E: E.activation(r2[:, :], r2[:, :], AF.Sqrt),
                   s_s, waits=[(s_v, q_t)])
        ab_guard = [(s_t, ST["last_ab"])] if ST.get("last_ab") else []
        if with_rms:
            sq2 = P.op("scalar", lambda E: E.activation(r3[:, :], r1[:, :], AF.Sqrt),
                       s_s)
            dn_t = P.op("vector", lambda E: E.scalar_tensor_tensor(
                r1[:, :], r2[:, :], 1e-8, r3[:, :], ALU.mult, ALU.add), s_v,
                waits=[(s_s, sq2)])
            a_t = P.op("vector", lambda E: E.reciprocal(a_row[:, :], r1[:, :]),
                       s_v, waits=[(s_v, dn_t)] + ab_guard)
        else:
            a_t = P.op("vector", lambda E: E.reciprocal(a_row[:, :], r2[:, :]),
                       s_v, waits=[(s_s, sq1)] + ab_guard)
        b_t = P.op("vector", lambda E: E.scalar_tensor_tensor(
            b_row[:, :], r0[:, :], -1.0, a_row[:, :], ALU.mult, ALU.mult), s_v,
            waits=[(s_v, a_t)])
        return {"rows": b_t, "m": m_t, "sq_mm": sq_mm}

    def emit_bcast_ab(pr, rows_tgt, xn_done_tgt, half=False):
        lhs = half_row_f if half else ones_row_f
        wts = [(s_v, rows_tgt)]
        if xn_done_tgt:
            wts.append((s_v, xn_done_tgt))
        P.op("tensor", lambda E, lhs=lhs: E.matmul(
            pBA[:, :], lhs[:, :], a_row[0:1, :], start=True, stop=True),
            s_t, waits=wts)
        tb = P.op("tensor", lambda E, lhs=lhs: E.matmul(
            pBB[:, :], lhs[:, :], b_row[0:1, :], start=True, stop=True), s_t)
        ST["last_ab"] = tb
        return tb

    def emit_xn(sub, b, bcast_tgt, x_waits):
        sl = slice(TB * b, TB * (b + 1))
        dst = xn2_sb if sub == 1 else xn_sb
        tmp = oflat if sub == 1 else t1_sb
        t = None
        for c in range(DC):
            t1t = P.op("vector", lambda E, c=c, sl=sl, tmp=tmp: E.tensor_tensor(
                tmp[:, :], x_sb[:, c, sl], pBA[:, :], ALU.mult), s_v,
                waits=([(s_t, bcast_tgt)] + list(x_waits)) if c == 0 else ())
            t = P.op("vector", lambda E, c=c, dst=dst, tmp=tmp: E.tensor_tensor(
                dst[:, c, :], tmp[:, :], pBB[:, :], ALU.add), s_v,
                waits=[(s_v, t1t)])
        return t

    pr = 0
    st0 = emit_stats(0, 0, pr, True, x_waits=[(s_x, x_tgt)],
                     extra_mm_waits=[(s_v, consts_done)])
    bc0 = emit_bcast_ab(pr, st0["rows"], None, half=True)
    last_xn = emit_xn(0, 0, bc0, x_waits=[(s_x, x_tgt)])

    g1 = dma("gpsimd", al_in[:, :], al_pack[:, :], waits=[(s_v, alred_done)])
    cc1 = P.op("gpsimd", lambda E: E.collective_compute(
        "AllReduce", ALU.add, replica_groups=[list(range(NCORES))],
        ins=[al_in.ap().opt()], outs=[al_out.ap().opt()]),
        s_cc, 1, waits=[(s_gq, g1)])
    g2 = dma("gpsimd", al_gath[:, :], al_out[:, :], waits=[(s_cc, cc1)])
    mm1 = P.op("tensor", lambda E: E.matmul(pS1[0:1, 0:2], sc_col[:, :],
                                            al_gath[:, :], start=True, stop=True),
               s_t, waits=[(s_gq, g2), (s_v, last_xn)])
    cp1 = P.op("scalar", lambda E: E.copy(th_src[:, :], pS1[0:1, 0:2]), s_s,
               waits=[(s_t, mm1)])
    mm2 = P.op("tensor", lambda E: E.matmul(pS2[:, 0:2], ones_row_f[:, :],
                                            th_src[0:1, :], start=True, stop=True),
               s_t, waits=[(s_s, cp1)])
    th_done = None
    for dst_, col_, sc_ in ((th_up_p, 0, 1.0), (th_up_n, 0, -1.0),
                            (th_dn_p, 1, 1.0), (th_dn_n, 1, -1.0)):
        th_done = P.op("vector", lambda E, d=dst_, c=col_, s=sc_: E.tensor_single_scalar(
            d[:, :], pS2[:, c:c + 1], s, ALU.mult), s_v, waits=[(s_t, mm2)])

    last_comb = {}
    slot_last = {}
    WSEMS = (s_w0, s_w1, s_w2, s_w3)
    PAIRS = ((tneg0, tpos0), (tneg1, tpos1), (tneg2, tpos2))
    tern_up_v = tern_up_g = None
    for m in range(128):
        up = m < 64
        mm_ = m if up else m - 64
        j = mm_ // 2
        half = mm_ % 2
        if up:
            src = wupT[128 * (j // 4):128 * (j // 4 + 1),
                       KIB * (j % 4) + TB * half:KIB * (j % 4) + TB * (half + 1)]
            dst8 = wup8
            slot = m % 4
            stage = wstage4
        else:
            src = wdnT[128 * j:128 * (j + 1), TB * half:TB * (half + 1)]
            dst8 = wdn8
            slot = 2 + (m % 2)
            stage = wstage4
        slot_sem = WSEMS[slot]
        wts = []
        prev = slot_last.get(slot)
        if prev is not None:
            wts.append(prev)
        if m < 4:
            wts.append((s_v, alred_done))
        ld = dma("sync", stage[:, slot, :], src, waits=wts, sem=slot_sem)
        eng = "gpsimd" if (m % 4 == 3) else "vector"
        csem = CSEM[eng]
        thp = th_up_p if up else th_dn_p
        thn = th_up_n if up else th_dn_n
        dst_sl = (dst8, j, TB * half, TB * (half + 1))
        tn, tp = PAIRS[m % 3]
        swts = [(slot_sem, ld), (s_v, th_done)]
        if last_comb.get(m % 3) is not None:
            swts.append(last_comb[m % 3])
        P.op("scalar", lambda E, slot=slot, thn=thn, tn=tn, stage=stage: E.activation(
            tn[:, :], stage[:, slot, :], AF.Sign, bias=thn[:, 0:1]),
            waits=swts)
        sg = P.op("scalar", lambda E, slot=slot, thp=thp, tp=tp, stage=stage:
                  E.activation(tp[:, :], stage[:, slot, :], AF.Sign,
                               bias=thp[:, 0:1]), s_s)
        tt = P.op(eng, lambda E, d=dst_sl, tn=tn, tp=tp: E.tensor_tensor(
            d[0][:, d[1], d[2]:d[3]], tn[:, :], tp[:, :], ALU.add), csem,
            waits=[(s_s, sg)])
        last_comb[m % 3] = (csem, tt)
        slot_last[slot] = (csem, tt)
        if up and mm_ == 63:
            tern_up_v = P.total(s_v)
            tern_up_g = P.total(s_g)
    tern_all_v = P.total(s_v)
    tern_all_g = P.total(s_g)

    ep_tgt = [[None] * DC for _ in range(3 * NB)]

    def emit_ffn_block(sub, b, gb, xn_tgt, next_stats):
        sl = slice(TB * b, TB * (b + 1))
        gelu_t = {}
        sq_t = {}
        hh_mm = {}
        upstop = {}
        cast_t = {}
        for fc in range(FC):
            eng = "vector" if fc % 2 == 0 else "gpsimd"
            csem = CSEM[eng]
            wts = []
            if upstop.get(fc - 2) is not None:
                wts.append((s_t, upstop[fc - 2]))
            if sub == 0:
                wts += [(s_v, tern_up_v), (s_g, tern_up_g)]
            else:
                wts.append((s_sp, ST["w8_reload_up"]))
            cast_t[fc] = (csem, P.op(eng, lambda E, fc=fc: E.tensor_copy(
                castup[:, fc % 2, :, :],
                wup8[:, (fc // 8)::4, 128 * (fc % 8):128 * (fc % 8 + 1)]),
                csem, waits=wts))
            w0 = [cast_t[fc], (s_v, xn_tgt)]
            if gelu_t.get(fc - 2) is not None:
                w0.append((s_s, gelu_t[fc - 2]))
            mm = None
            for dc in range(DC):
                mm = P.op("tensor", lambda E, fc=fc, dc=dc: E.matmul(
                    pUP[fc % 2][:, :], castup[:, fc % 2, dc, :], xn_sb[:, dc, :],
                    start=(dc == 0), stop=(dc == DC - 1)),
                    s_t if dc == DC - 1 else None, waits=w0 if dc == 0 else ())
            upstop[fc] = mm
            gelu_t[fc] = P.op("scalar", lambda E, fc=fc: E.activation(
                h_sb[:, fc, :], pUP[fc % 2][:, :], AF.Gelu), s_s,
                waits=[(s_t, upstop[fc])])
            if fc >= 2:
                f2 = fc - 2
                hh_mm[f2] = P.op("tensor", lambda E, f2=f2: E.matmul(
                    pS1[0:1, :], ones_col_bf[:, :], hsq_sb[:, f2 % 2, :],
                    start=(f2 == 0), stop=(f2 == FC - 1)),
                    s_t, waits=[(s_s, sq_t[f2])])
            sqw = [(s_s, gelu_t[fc])]
            if hh_mm.get(fc - 2) is not None:
                sqw.append((s_t, hh_mm[fc - 2]))
            if sub == 0 and b == 0 and fc < 2:
                sqw += [(s_v, tern_all_v), (s_g, tern_all_g)]
            sq_t[fc] = P.op("scalar", lambda E, fc=fc: E.activation(
                hsq_sb[:, fc % 2, :], h_sb[:, fc, :], AF.Square), s_s, waits=sqw)
        for f2 in (FC - 2, FC - 1):
            hh_mm[f2] = P.op("tensor", lambda E, f2=f2: E.matmul(
                pS1[0:1, :], ones_col_bf[:, :], hsq_sb[:, f2 % 2, :],
                start=(f2 == 0), stop=(f2 == FC - 1)),
                s_t, waits=[(s_s, sq_t[f2])])
        hh_done = hh_mm[FC - 1]
        cpre = P.op("vector", lambda E: E.tensor_single_scalar(
            r3[:, :], pS1[0:1, :], 1.0 / DF, ALU.mult), s_v,
            waits=[(s_t, hh_done)])
        csq = P.op("scalar", lambda E: E.activation(r3[:, :], r3[:, :], AF.Sqrt),
                   s_s, waits=[(s_v, cpre)])
        ct1 = P.op("vector", lambda E: E.tensor_single_scalar(
            r3[:, :], r3[:, :], 1e-8, ALU.add), s_v, waits=[(s_s, csq)])
        cb_guard = [(s_t, ST["last_cb"])] if ST.get("last_cb") else []
        ct2 = P.op("vector", lambda E: E.reciprocal(c_row[:, :], r3[:, :]), s_v,
                   waits=cb_guard + [(s_v, ct1)])
        st_info = None
        if next_stats is not None:
            st_info = next_stats(extra_mm_waits=[(s_s, csq)])
        cwts = [(s_v, ct2)]
        if st_info is not None:
            cwts.append((s_v, st_info["m"]))
        cb = P.op("tensor", lambda E: E.matmul(pS1[:, :], quar_row_f[:, :],
                                               c_row[0:1, :], start=True, stop=True),
                  s_t, waits=cwts)
        ST["last_cb"] = cb
        cwts2 = [(s_t, cb)]
        if sub == 0 and b == 0:
            cwts2 += [(s_v, tern_all_v), (s_g, tern_all_g)]
        csb = P.op("scalar", lambda E: E.copy(csb_sb[:, :], pS1[:, :]), s_s,
                   waits=cwts2)
        dnstop = {}
        castd_t = {}
        qend = {}
        ep_list = []
        def emit_castd(dc, q):
            key = 4 * dc + q
            eng = "vector" if key % 2 == 0 else "gpsimd"
            csem = CSEM[eng]
            wts = []
            if qend.get(key - 2) is not None:
                wts.append((s_t, qend[key - 2]))
            if sub == 0 and b == 0 and key < 2:
                wts += [(s_v, tern_all_v), (s_g, tern_all_g)]
            if sub == 2 and key < 2:
                wts.append((s_sp, ST["w8_reload_dn"]))
            castd_t[key] = (csem, P.op(eng, lambda E, q=q, dc=dc, key=key:
                                       E.tensor_copy(
                castdn[:, key % 2, :, :],
                wdn8[:, 8 * q:8 * (q + 1), 128 * dc:128 * (dc + 1)]),
                csem, waits=wts))

        for dc in range(DC):
            emit_castd(dc, 0)
            emit_castd(dc, 1)
            w0 = [(s_s, gelu_t[FC - 1])]
            if dc >= 2 and ep_tgt[gb][dc - 2] is not None:
                w0.append((s_v, ep_tgt[gb][dc - 2]))
            mm = None
            for fc in range(FC):
                if fc == 8:
                    emit_castd(dc, 2)
                if fc == 16:
                    emit_castd(dc, 3)
                fw = list(w0) if fc == 0 else []
                if fc % 8 == 0:
                    fw.append(castd_t[4 * dc + fc // 8])
                mm = P.op("tensor", lambda E, dc=dc, fc=fc: E.matmul(
                    pDN[dc % 2][:, :], castdn[:, (4 * dc + fc // 8) % 2, fc % 8, :],
                    h_sb[:, fc, :], start=(fc == 0), stop=(fc == FC - 1)),
                    s_t if fc % 8 == 7 else None, waits=fw)
                if fc % 8 == 7:
                    qend[4 * dc + fc // 8] = mm
            dnstop[dc] = mm
            e1 = P.op("vector", lambda E, dc=dc: E.tensor_tensor(
                t1_sb[:, :], pDN[dc % 2][:, :], csb_sb[:, :], ALU.mult), s_v,
                waits=[(s_t, dnstop[dc]), (s_s, csb)])
            ep = P.op("vector", lambda E, dc=dc, sl=sl: E.tensor_tensor(
                x_sb[:, dc, sl], x_sb[:, dc, sl], t1_sb[:, :], ALU.add), s_v,
                waits=[(s_v, e1)])
            ep_tgt[gb][dc] = ep
            ep_list.append(ep)
        return ep_list, st_info

    def emit_ssm_block(b, gb, xn_tgt, next_stats, prev_guard):
        sl = slice(TB * b, TB * (b + 1))
        pc = b % 2
        cpc_o, cpc_n = cpc_pp[pc], cpc_pp[1 - pc]
        stc_o, stc_n = stc_pp[pc], stc_pp[1 - pc]
        evd = {}
        evi = {}
        decstop = {}
        injstop = {}
        for ec in range(DC):
            wts = [(s_gq, ST["wdec_ld"])] if ec == 0 else []
            if ec == 0:
                wts.append((s_v, xn_tgt))
            if evd.get(ec - 2) is not None:
                wts.append((s_s, evd[ec - 2]))
            mm = None
            for dc in range(DC):
                mm = P.op("tensor", lambda E, ec=ec, dc=dc: E.matmul(
                    pUP[ec % 2][:, :], wdec_sb[:, dc, 128 * ec:128 * (ec + 1)],
                    xn2_sb[:, dc, :], start=(dc == 0), stop=(dc == DC - 1)),
                    s_t if dc == DC - 1 else None, waits=wts if dc == 0 else ())
            decstop[ec] = mm
            ewts = [(s_t, decstop[ec]), (s_aq, ST["bias_ld"])]
            if prev_guard is not None:
                ewts.append((s_v, prev_guard["dec"][ec]))
            evd[ec] = P.op("scalar", lambda E, ec=ec: E.activation(
                dec_sb[:, ec, :], pUP[ec % 2][:, :], AF.Sigmoid,
                bias=bdec_sb[:, ec:ec + 1]), s_s, waits=ewts)
        for ec in range(DC):
            wts = [(s_gq, ST["winp_ld"])] if ec == 0 else []
            if evi.get(ec - 2) is not None:
                wts.append((s_s, evi[ec - 2]))
            mm = None
            for dc in range(DC):
                mm = P.op("tensor", lambda E, ec=ec, dc=dc: E.matmul(
                    pDN[ec % 2][:, :], winp_sb[:, dc, 128 * ec:128 * (ec + 1)],
                    xn2_sb[:, dc, :], start=(dc == 0), stop=(dc == DC - 1)),
                    s_t if dc == DC - 1 else None, waits=wts if dc == 0 else ())
            injstop[ec] = mm
            ewts = [(s_t, injstop[ec]), (s_aq, ST["bias_ld"])]
            if prev_guard is not None:
                ewts.append((s_v, prev_guard["inj"][ec]))
            evi[ec] = P.op("scalar", lambda E, ec=ec: E.activation(
                inj_sb[:, ec, :], pDN[ec % 2][:, :], AF.Identity,
                bias=binp_sb[:, ec:ec + 1]), s_s, waits=ewts)
        cp_t = {}
        for ec in range(DC):
            cp_t[ec] = P.op("vector", lambda E, ec=ec: E.tensor_tensor_scan(
                cp32_sb[:, ec, :], dec_sb[:, ec, :], zeros_sb[:, :], 1.0,
                ALU.mult, ALU.add), s_v, waits=[(s_s, evd[ec])])
        cpf_w = list(ST["cpf_guard"]) if b == 0 else []
        cpf = None
        for ec in range(DC):
            cpf = P.op("vector", lambda E, ec=ec, sl=sl: E.tensor_single_scalar(
                cp_full[:, ec, sl], cp32_sb[:, ec, :], cpc_o[:, ec:ec + 1],
                ALU.mult), s_v,
                waits=(cpf_w if ec == 0 else []) + [(s_v, cp_t[ec])])
        cpn = P.op("vector", lambda E: E.tensor_tensor(
            cpc_n[:, :], cp32_sb[:, :, TB - 1:TB], cpc_o[:, :], ALU.mult), s_v,
            waits=[(s_v, cpf)])
        spacer("vector")
        dec_guard = {}
        inj_guard = {}
        ep = None
        for ec in range(DC):
            inj_guard[ec] = P.op("vector", lambda E, ec=ec: E.tensor_copy(
                iflat[:, :], inj_sb[:, ec, :]), s_v, waits=[(s_s, evi[ec])])
            dec_guard[ec] = P.op("vector", lambda E, ec=ec: E.tensor_tensor_scan(
                oflat[:, :], dec_sb[:, ec, :], iflat[:, :], 0.0,
                ALU.mult, ALU.add), s_v, waits=[(s_v, inj_guard[ec])])
            sfx = P.op("vector", lambda E, ec=ec: E.scalar_tensor_tensor(
                iflat[:, :], cp32_sb[:, ec, :], stc_o[:, ec:ec + 1], oflat[:, :],
                ALU.mult, ALU.add), s_v, waits=[(s_v, dec_guard[ec])])
            ep = P.op("vector", lambda E, ec=ec, sl=sl: E.tensor_tensor(
                x_sb[:, ec, sl], x_sb[:, ec, sl], iflat[:, :], ALU.add), s_v,
                waits=[(s_v, sfx)])
            P.op("vector", lambda E, ec=ec: E.tensor_copy(
                stc_n[:, ec:ec + 1], iflat[:, TB - 1:TB]), s_v, waits=[(s_v, ep)])
        st_info = next_stats(extra_mm_waits=()) if next_stats is not None else None
        for dc in range(DC):
            ep_tgt[gb][dc] = ep
        return {"ep": ep, "cpf": cpf, "cpn": cpn, "st_info": st_info,
                "guard": {"dec": dec_guard, "inj": inj_guard},
                "carry_new": 1 - pc}

    NBF = debug_nb if sublayers == 1 else NB
    for sub in [0, 1, 2][:sublayers]:
        if sub in (0, 2):
            for b in range(NBF):
                gb = (0 if sub == 0 else 2 * NB) + b
                npr = 1 - pr
                nxt = None
                if b < NBF - 1:
                    if sub == 0:
                        nxt = (lambda extra_mm_waits, b=b, npr=npr: emit_stats(
                            0, b + 1, npr, True, x_waits=[(s_x, x_tgt)],
                            extra_mm_waits=extra_mm_waits))
                    else:
                        nxt = (lambda extra_mm_waits, b=b, npr=npr: emit_stats(
                            2, b + 1, npr, True, x_waits=[(s_v, ST["fix_done"])],
                            extra_mm_waits=extra_mm_waits))

                eps, st_info = emit_ffn_block(sub, b, gb, last_xn, nxt)
                if st_info is not None:
                    nsub = sub if b < NB - 1 else 1
                    nb = b + 1 if b < NB - 1 else 0
                    bc = emit_bcast_ab(npr, st_info["rows"], last_xn, half=(nsub != 1))
                    last_xn = emit_xn(nsub, nb, bc, x_waits=[(s_v, eps[-1])])
                    pr = npr
                if sub == 2:
                    yr = yT.rearrange("(c p) t -> p c t", p=128)
                    sl = slice(TB * b, TB * (b + 1))
                    ST["store"] = dma("scalar", yr[:, :, sl], x_sb[:, :, sl],
                                      waits=[(s_v, eps[-1])])
        else:
            spill_up = dma("sync", sp_up[:, :, :], wup8[:, :, :],
                           waits=[(s_v, tern_all_v), (s_g, tern_all_g)], sem=s_sp)
            wdec_ld = dma("gpsimd", wdec_sb[:, :, :],
                          wdecT.rearrange("(c p) e -> p c e", p=128),
                          waits=[(s_sp, spill_up), (s_v, P.total(s_v)),
                                 (s_g, P.total(s_g))])
            winp_ld = dma("gpsimd", winp_sb[:, :, :],
                          winpT.rearrange("(c p) e -> p c e", p=128))
            ST["wdec_ld"] = winp_ld
            ST["winp_ld"] = winp_ld
            spill_dn = dma("sync", sp_dn[:, :, :], wdn8[:, :, :],
                           waits=[(s_sp, spill_up), (s_v, P.total(s_v)),
                                  (s_g, P.total(s_g))], sem=s_sp)
            ST["cpf_guard"] = [(s_sp, spill_dn)]
            npr = 1 - pr
            st1 = emit_stats(1, 0, npr, False,
                             x_waits=[(s_v, ep_tgt[NB - 1][DC - 1])])
            bc1 = emit_bcast_ab(npr, st1["rows"], last_xn, half=False)
            last_xn = emit_xn(1, 0, bc1, x_waits=[(s_v, ep_tgt[NB - 1][DC - 1])])
            pr = npr
            res = None
            prev_guard = None
            for b in range(NB):
                gb = NB + b
                npr = 1 - pr
                nxt = None
                if b < NB - 1:
                    nxt = (lambda extra_mm_waits, b=b, npr=npr: emit_stats(
                        1, b + 1, npr, False,
                        x_waits=[(s_v, ep_tgt[b + 1][DC - 1])],
                        extra_mm_waits=extra_mm_waits))
                res = emit_ssm_block(b, gb, last_xn, nxt, prev_guard)
                prev_guard = res["guard"]
                if res["st_info"] is not None:
                    bc = emit_bcast_ab(npr, res["st_info"]["rows"], last_xn, half=False)
                    last_xn = emit_xn(1, b + 1, bc, x_waits=[(s_v, res["ep"])])
                    pr = npr
            cnew = res["carry_new"]
            pk1 = P.op("vector", lambda E: E.tensor_copy(
                exch_sb[:, 0:8], cpc_pp[cnew][:, :]), s_v, waits=[(s_v, res["cpn"])])
            pk2 = P.op("vector", lambda E: E.tensor_copy(
                exch_sb[:, 8:16], stc_pp[cnew][:, :]), s_v, waits=[(s_v, res["ep"])])
            ge = dma("gpsimd", ag_in[:, :], exch_sb[:, :], waits=[(s_v, pk2)])
            cct = P.op("gpsimd", lambda E: E.collective_compute(
                "AllGather", ALU.bypass, replica_groups=[list(range(NCORES))],
                ins=[ag_in.ap().opt()], outs=[ag_out.ap().opt()]),
                s_cc, 1, waits=[(s_gq, ge)])
            gg = dma("gpsimd", gath_sb[:, :, :],
                     ag_out.ap().rearrange("j p k -> p j k"), waits=[(s_cc, cct)])
            pt = P.op("vector", lambda E: E.tensor_copy(S_pp[0][:, :],
                                                        gath_sb[:, 0, 8:16]), s_v,
                      waits=[(s_gq, gg), (s_aq, misc_tgt)])
            spacer("vector")
            P.op("vector", lambda E: E.tensor_copy(cs_sb[:, :, 0], S_pp[0][:, :]),
                 s_v, waits=[(s_v, pt)])
            sj = 0
            for j in range(1, 8):
                spacer("vector")
                pt = P.op("vector", lambda E, j=j, sj=sj: E.tensor_tensor(
                    tmpS[:, :], gath_sb[:, j, 0:8], S_pp[sj][:, :], ALU.mult), s_v,
                    waits=[(s_v, pt)])
                spacer("vector")
                pt = P.op("vector", lambda E, j=j, sj=sj: E.tensor_tensor(
                    S_pp[1 - sj][:, :], tmpS[:, :], gath_sb[:, j, 8:16], ALU.add),
                    s_v, waits=[(s_v, pt)])
                sj = 1 - sj
                spacer("vector")
                pt = P.op("vector", lambda E, j=j, sj=sj: E.tensor_copy(
                    cs_sb[:, :, j], S_pp[sj][:, :]), s_v, waits=[(s_v, pt)])
            spacer("vector")
            ml = P.op("vector", lambda E: E.tensor_tensor(
                tmp88_sb[:, :, :], cs_sb[:, :, :], selm_sb[:, :, :], ALU.mult), s_v,
                waits=[(s_v, pt)])
            spacer("vector")
            rd = None
            for c in range(DC):
                rd = P.op("vector", lambda E, c=c: E.tensor_reduce(
                    sin_sb[:, c:c + 1], tmp88_sb[:, c, :], AX.X, ALU.add), s_v,
                    waits=[(s_v, ml)] if c == 0 else ())
            spacer("vector")
            fx = None
            for c in range(DC):
                fx = P.op("vector", lambda E, c=c: E.scalar_tensor_tensor(
                    x_sb[:, c, :], cp_full[:, c, :], sin_sb[:, c:c + 1],
                    x_sb[:, c, :], ALU.mult, ALU.add), s_v,
                    waits=[(s_v, rd), (s_v, res["cpf"])] if c == 0 else ())
            ST["fix_done"] = fx
            if sublayers > 2:
                rl_up = dma("sync", wup8[:, :, :], sp_up[:, :, :],
                            waits=[(s_t, P.total(s_t)), (s_gq, P.total(s_gq)),
                                   (s_sp, spill_dn)], sem=s_sp)
                rl_dn = dma("sync", wdn8[:, :, :], sp_dn[:, :, :],
                            waits=[(s_v, fx), (s_sp, rl_up)], sem=s_sp)
                ST["w8_reload_up"] = rl_up
                ST["w8_reload_dn"] = rl_dn
                npr = 1 - pr
                st2 = emit_stats(2, 0, npr, True, x_waits=[(s_v, fx)])
                bc2 = emit_bcast_ab(npr, st2["rows"], last_xn, half=True)
                last_xn = emit_xn(2, 0, bc2, x_waits=[(s_v, fx)])
                pr = npr

    if sublayers < 3:
        yr = yT.rearrange("(c p) t -> p c t", p=128)
        dma("scalar", yr[:, :, :], x_sb[:, :, :],
            waits=[(s_v, P.total(s_v)), (s_s, P.total(s_s)), (s_g, P.total(s_g))])
    if sublayers == 2:
        for nm, t_, shp in (
            ("s_dec", dec_sb, [128, DC, TB]), ("s_inj", inj_sb, [128, DC, TB]),
            ("s_cp32", cp32_sb, [128, DC, TB]), ("s_sin", sin_sb, [128, DC]),
            ("s_gath", gath_sb, [128, 8, 16]), ("s_exch", exch_sb, [128, 16]),
            ("s_cpc", cpc_pp[0], [128, DC]), ("s_stc", stc_pp[0], [128, DC])):
            dr = nc.dram_tensor(nm, shp, F32, kind="ExternalOutput").ap()
            dma("scalar", dr, t_[tuple(slice(None) for _ in shp)],
                waits=[(s_v, P.total(s_v)), (s_s, P.total(s_s)),
                       (s_g, P.total(s_g)), (s_t, P.total(s_t))])
        drc = nc.dram_tensor("s_cpf", [128, DC, TC], BF16, kind="ExternalOutput").ap()
        dma("scalar", drc, cp_full[:, :, :],
            waits=[(s_v, P.total(s_v))])
        drx = nc.dram_tensor("s_xn2", [128, DC, TB], BF16, kind="ExternalOutput").ap()
        dma("scalar", drx, xn2_sb[:, :, :], waits=[(s_v, P.total(s_v))])
    if sublayers == 1 and debug_nb == 1:
        dbg = {}
        for nm, t_, shp, dt_ in (
            ("d_wup8", wup8, [128, FC, KIB], F8), ("d_wdn8", wdn8, [128, FC, KIB], F8),
            ("d_xn", xn_sb, [128, DC, TB], BF16), ("d_h", h_sb, [128, FC, TB], BF16),
            ("d_csb", csb_sb, [128, TB], F32), ("d_arow", a_row, [1, TB], F32),
            ("d_brow", b_row, [1, TB], F32), ("d_crow", c_row, [1, TB], F32),
            ("d_thup", th_up_p, [128, 1], F32), ("d_thdn", th_dn_p, [128, 1], F32),
            ("d_alg", al_gath, [128, 2], F32), ("d_alp", al_pack, [128, 2], F32)):
            dr = nc.dram_tensor(nm, shp, dt_, kind="ExternalOutput").ap()
            dma("scalar", dr, t_[tuple(slice(None) for _ in shp)],
                waits=[(s_v, P.total(s_v)), (s_s, P.total(s_s)),
                       (s_g, P.total(s_g)), (s_t, P.total(s_t))])

    P.op("sync", lambda E: E.nop(), waits=[(s_aq, P.total(s_aq)),
                                           (s_dq, P.total(s_dq)),
                                           (s_gq, P.total(s_gq)),
                                           (s_x, P.total(s_x)),
                                           (s_w0, P.total(s_w0)),
                                           (s_w1, P.total(s_w1)),
                                           (s_w2, P.total(s_w2)),
                                           (s_w3, P.total(s_w3)),
                                           (s_sp, P.total(s_sp))])

    block = es.enter_context(nc.Block())

    def runner(oplist):
        def f(E):
            for fn in oplist:
                fn(E)
        return f

    block.tensor(runner(P.ops["tensor"]))
    block.scalar(runner(P.ops["scalar"]))
    block.vector(runner(P.ops["vector"]))
    block.gpsimd(runner(P.ops["gpsimd"]))
    block.sync(runner(P.ops["sync"]))
    es.close()
    return nc


def kernel(L0_attn_x_seq, L0_attn_W_decay, L0_attn_b_decay, L0_attn_W_input,
           L0_attn_b_input, L0_ffn_w_up, L0_ffn_w_down, sublayers=3, debug_nb=NB,
           _dbg_out=None):
    x = np.asarray(L0_attn_x_seq)[0]
    xT = np.ascontiguousarray(x.T.astype(np.float32))
    wupT = np.ascontiguousarray(np.asarray(L0_ffn_w_up).T.astype(np.float32))
    wdnT = np.ascontiguousarray(np.asarray(L0_ffn_w_down).T.astype(np.float32))
    wdecT = np.ascontiguousarray(np.asarray(L0_attn_W_decay).T.astype(np.float32))
    winpT = np.ascontiguousarray(np.asarray(L0_attn_W_input).T.astype(np.float32))
    bdec = np.ascontiguousarray(
        np.asarray(L0_attn_b_decay).reshape(DC, 128).T.astype(np.float32))
    binp = np.ascontiguousarray(
        np.asarray(L0_attn_b_input).reshape(DC, 128).T.astype(np.float32))

    nc = build_nc(sublayers=sublayers, debug_nb=debug_nb)
    in_maps = []
    for i in range(NCORES):
        sel = np.zeros((128, DC, 8), np.float32)
        if i > 0:
            sel[:, :, i - 1] = 1.0
        in_maps.append({
            "xT": np.ascontiguousarray(xT[:, i * TC:(i + 1) * TC]),
            "wupT": wupT, "wdnT": wdnT, "wdecT": wdecT, "winpT": winpT,
            "wup_sl": np.ascontiguousarray(wupT[128 * i:128 * (i + 1), :]),
            "wdn_sl": np.ascontiguousarray(wdnT[512 * i:512 * (i + 1), :]),
            "bdec": bdec, "binp": binp, "selm": sel,
        })
    res = run_bass_kernel_spmd(nc, in_maps, core_ids=list(range(NCORES)))
    if _dbg_out is not None:
        for r in res.results:
            _dbg_out.append({k_: np.asarray(v_) for k_, v_ in r.items()})
    yT_full = np.concatenate([r["yT"] for r in res.results], axis=1)
    return np.ascontiguousarray(yT_full.T)[None].astype(np.float32)


# version 05
# speedup vs baseline: 1.0406x; 1.0406x over previous
import numpy as np
from contextlib import ExitStack

import concourse.bass as bass
import concourse.mybir as mybir
from concourse.bass_utils import run_bass_kernel_spmd

F32 = mybir.dt.float32
BF16 = mybir.dt.bfloat16
F8 = mybir.dt.float8e4
U8 = mybir.dt.uint8
AF = mybir.ActivationFunctionType
ALU = mybir.AluOpType
AX = mybir.AxisListType

NCORES = 8
D = 1024
DC = 8
DF = 4096
FC = 32
TC = 2048
NB = 4
TB = 512
KIB = 1024

ENGINES = ("tensor", "scalar", "vector", "gpsimd", "sync")


class Planner:
    def __init__(self):
        self.ops = {e: [] for e in ENGINES}
        self.cnt = {}
        self.lastw = {e: {} for e in ENGINES}
        self.vsem = None

    def total(self, sem):
        return self.cnt.get(sem.name, 0)

    def wait(self, eng, sem, val):
        if sem is None or val is None or val <= 0:
            return
        if self.lastw[eng].get(sem.name, 0) >= val:
            return
        self.lastw[eng][sem.name] = val
        self.ops[eng].append(lambda E, s=sem, v=val: E.wait_ge(s, v))

    def op(self, eng, build, sem=None, n=1, waits=()):
        if eng == "vector" and self.vsem is not None:
            if sem is None:
                sem = self.vsem
            if sem is self.vsem:
                prev = self.cnt.get(sem.name, 0)
                if prev > 0:
                    self.wait(eng, sem, prev)
        for s, v in waits:
            self.wait(eng, s, v)
        if sem is None:
            self.ops[eng].append(build)
            return None
        self.cnt[sem.name] = self.cnt.get(sem.name, 0) + n
        tgt = self.cnt[sem.name]
        self.ops[eng].append(lambda E, b=build, s=sem, k=n: b(E).then_inc(s, k))
        return tgt


def build_nc(sublayers=3, debug_nb=NB):
    nc = bass.Bass("TRN2", target_bir_lowering=False, debug=False)

    xT = nc.dram_tensor("xT", [D, TC], F32, kind="ExternalInput").ap()
    wupT = nc.dram_tensor("wupT", [D, DF], F32, kind="ExternalInput").ap()
    wdnT = nc.dram_tensor("wdnT", [DF, D], F32, kind="ExternalInput").ap()
    wdecT = nc.dram_tensor("wdecT", [D, D], F32, kind="ExternalInput").ap()
    winpT = nc.dram_tensor("winpT", [D, D], F32, kind="ExternalInput").ap()
    wup_sl = nc.dram_tensor("wup_sl", [128, DF], F32, kind="ExternalInput").ap()
    wdn_sl = nc.dram_tensor("wdn_sl", [512, D], F32, kind="ExternalInput").ap()
    bdec = nc.dram_tensor("bdec", [128, DC], F32, kind="ExternalInput").ap()
    binp = nc.dram_tensor("binp", [128, DC], F32, kind="ExternalInput").ap()
    selm = nc.dram_tensor("selm", [128, DC, 8], F32, kind="ExternalInput").ap()
    yT = nc.dram_tensor("yT", [D, TC], F32, kind="ExternalOutput").ap()
    al_in = nc.dram_tensor("al_in", [128, 2], F32)
    al_out = nc.dram_tensor("al_out", [128, 2], F32)
    ag_in = nc.dram_tensor("ag_in", [128, 16], F32)
    ag_out = nc.dram_tensor("ag_out", [8, 128, 16], F32)
    sp_up = nc.dram_tensor("sp_up", [128, FC, KIB], F8)
    sp_dn = nc.dram_tensor("sp_dn", [128, FC, KIB], F8)

    es = ExitStack()

    def sb(name, shape, dtype):
        return es.enter_context(nc.sbuf_tensor(name, shape, dtype))

    x_sb = sb("x_sb", [128, DC, TC], F32)
    ones_col_bf = sb("ones_col_bf", [128, 1], BF16)
    ones_row_f = sb("ones_row_f", [1, 128], F32)
    half_row_f = sb("half_row_f", [1, 128], F32)
    quar_row_f = sb("quar_row_f", [1, 128], F32)
    bdec_sb = sb("bdec_sb", [128, DC], F32)
    binp_sb = sb("binp_sb", [128, DC], F32)
    selm_sb = sb("selm_sb", [128, DC, 8], F32)
    exch_sb = sb("exch_sb", [128, 16], F32)
    gath_sb = sb("gath_sb", [128, 8, 16], F32)
    cs_sb = sb("cs_sb", [128, DC, 8], F32)
    tmp88_sb = sb("tmp88_sb", [128, DC, 8], F32)
    sin_sb = sb("sin_sb", [128, DC], F32)
    cpc_pp = [sb("cpc_a", [128, DC], F32), sb("cpc_b", [128, DC], F32)]
    stc_pp = [sb("stc_a", [128, DC], F32), sb("stc_b", [128, DC], F32)]
    iflat = sb("iflat", [128, TB], F32)
    oflat = sb("oflat", [128, TB], F32)
    S_pp = [sb("S_a", [128, DC], F32), sb("S_b", [128, DC], F32)]
    tmpS = sb("tmpS", [128, DC], F32)
    al_pack = sb("al_pack", [128, 2], F32)
    acc16 = sb("acc16", [128, 16], F32)
    zeros_sb = sb("zeros_sb", [128, TB], F32)
    sc_col = sb("sc_col", [128, 1], F32)
    spc_sb = sb("spc_sb", [1, 8], F32)
    al_gath = sb("al_gath", [128, 2], F32)
    th_src = sb("th_src", [1, 2], F32)
    th_up_p = sb("th_up_p", [128, 1], F32)
    th_up_n = sb("th_up_n", [128, 1], F32)
    th_dn_p = sb("th_dn_p", [128, 1], F32)
    th_dn_n = sb("th_dn_n", [128, 1], F32)
    a_row = sb("a_row", [1, TB], F32)
    b_row = sb("b_row", [1, TB], F32)
    c_row = sb("c_row", [1, TB], F32)
    r0 = sb("r0", [1, TB], F32)
    r1 = sb("r1", [1, TB], F32)
    r2 = sb("r2", [1, TB], F32)
    r3 = sb("r3", [1, TB], F32)

    ARENA = 120 * KIB
    arena = es.enter_context(nc.sbuf_tensor("arena", [128, ARENA], U8))
    arena_base = nc.lookup_mloc(arena).addr

    def at(off_kib, name, shape, dtype):
        hnd = nc.virtual_sbuf_tensor(name, shape, dtype)
        nc._tensor(hnd.name, hnd.shape, hnd.dtype, type="SB")
        mloc = nc.lookup_mloc(hnd)
        mloc.allocated = True
        mloc.addr = arena_base + int(off_kib * KIB)
        return hnd

    wup8 = at(0, "wup8", [128, FC, KIB], F8)
    wdn8 = at(32, "wdn8", [128, FC, KIB], F8)
    h_sb = at(64, "h_sb", [128, FC, TB], BF16)
    xn_sb = at(96, "xn_sb", [128, DC, TB], BF16)
    xbf_sb = at(96, "xbf_sb", [128, DC, TB], BF16)
    castup = at(104, "castup", [128, 2, DC, 128], BF16)
    castdn = at(108, "castdn", [128, 2, DC, 128], BF16)
    t1_sb = at(112, "t1_sb", [128, TB], F32)
    csb_sb = at(114, "csb_sb", [128, TB], F32)
    hsq_sb = at(116, "hsq_sb", [128, 2, TB], BF16)
    wstage = at(108, "wstage", [128, 2, TB], F32)
    wstage4 = at(104, "wstage4", [128, 4, TB], F32)
    tneg2 = at(114, "tneg2", [128, TB], BF16)
    tpos2 = at(115, "tpos2", [128, TB], BF16)
    tneg0 = at(116, "tneg0", [128, TB], BF16)
    tpos0 = at(117, "tpos0", [128, TB], BF16)
    tneg1 = at(118, "tneg1", [128, TB], BF16)
    tpos1 = at(119, "tpos1", [128, TB], BF16)
    wdec_sb = at(0, "wdec_sb", [128, DC, KIB], BF16)
    winp_sb = at(16, "winp_sb", [128, DC, KIB], BF16)
    cp_full = at(32, "cp_full", [128, DC, TC], BF16)
    dec_sb = at(64, "dec_sb", [128, DC, TB], F32)
    inj_sb = at(80, "inj_sb", [128, DC, TB], F32)
    cp32_sb = at(96, "cp32_sb", [128, DC, TB], F32)
    xn2_sb = at(112, "xn2_sb", [128, DC, TB], BF16)
    xbf2_sb = at(112, "xbf2_sb", [128, DC, TB], BF16)

    def psum(name):
        return es.enter_context(nc.psum_tensor(name, [128, TB], F32))

    pUP = [psum("pUP0"), psum("pUP1")]
    pDN = [psum("pDN0"), psum("pDN1")]
    pS1 = psum("pS1")
    pS2 = psum("pS2")
    pBA = psum("pBA")
    pBB = psum("pBB")

    def sem(name):
        return es.enter_context(nc.semaphore(name))

    s_dq = sem("s_dq")
    s_aq = sem("s_aq")
    s_gq = sem("s_gq")
    s_x = sem("s_x")
    s_w0 = sem("s_w0")
    s_w1 = sem("s_w1")
    s_w2 = sem("s_w2")
    s_w3 = sem("s_w3")
    s_sp = sem("s_sp")
    s_t = sem("s_t")
    s_s = sem("s_s")
    s_v = sem("s_v")
    s_g = sem("s_g")
    s_cc = sem("s_cc")

    P = Planner()
    P.vsem = s_v
    DMAS = {"sync": s_dq, "scalar": s_aq, "gpsimd": s_gq}
    CSEM = {"vector": s_v, "gpsimd": s_g, "scalar": s_s, "tensor": s_t}
    ST = {}

    def spacer(eng):
        P.op(eng, lambda E: E.tensor_single_scalar(spc_sb[:, :], spc_sb[:, :], 0.0,
                                                   ALU.add))

    def dma(eng, out_ap, in_ap, waits=(), sem=None):
        return P.op(eng, lambda E, o=out_ap, i=in_ap: E.dma_start(out=o, in_=i),
                    sem if sem is not None else DMAS[eng], 16, waits)

    xr = xT.rearrange("(c p) t -> p c t", p=128)
    x_tgt = None
    for i in range(4):
        x_tgt = dma("scalar", x_sb[:, 2 * i:2 * i + 2, :], xr[:, 2 * i:2 * i + 2, :],
                    sem=s_x)
    dma("scalar", bdec_sb[:, :], bdec)
    binp_tgt = dma("scalar", binp_sb[:, :], binp)
    misc_tgt = dma("scalar", selm_sb[:, :, :], selm)
    ST["x"] = x_tgt
    ST["bias_ld"] = misc_tgt

    for t_, v_ in ((ones_col_bf, 1.0), (ones_row_f, 1.0), (half_row_f, 0.5),
                   (quar_row_f, 0.25),
                   (al_pack, 0.0), (zeros_sb, 0.0), (cpc_pp[0], 1.0),
                   (stc_pp[0], 0.0), (sc_col, 0.5 / (D * DF)), (spc_sb, 0.0)):
        consts_done = P.op("vector", lambda E, t=t_, v=v_: E.memset(t[:, :], v), s_v)

    sl_tgts = []
    red_tgts = []
    for k in range(16):
        if k < 8:
            src = wup_sl[:, TB * k:TB * (k + 1)]
        else:
            kk = k - 8
            src = wdn_sl[128 * (kk // 2):128 * (kk // 2 + 1),
                         TB * (kk % 2):TB * (kk % 2 + 1)]
        wts = [(s_v, red_tgts[k - 2])] if k >= 2 else []
        slot_sem = s_w0 if k % 2 == 0 else s_w1
        sl_tgts.append(dma("sync", wstage[:, k % 2, :], src, waits=wts, sem=slot_sem))
        red_tgts.append(P.op("vector", lambda E, k=k: E.tensor_reduce(
            acc16[:, k:k + 1], wstage[:, k % 2, :], AX.X, ALU.add,
            apply_absolute_value=True),
            s_v, waits=[(slot_sem, sl_tgts[k]), (s_v, consts_done)]))
    spacer("vector")
    r_up = P.op("vector", lambda E: E.tensor_reduce(al_pack[:, 0:1], acc16[:, 0:8],
                                                    AX.X, ALU.add), s_v,
                waits=[(s_v, red_tgts[-1])])
    alred_done = P.op("vector", lambda E: E.tensor_reduce(
        al_pack[:, 1:2], acc16[:, 8:16], AX.X, ALU.add), s_v,
        waits=[(s_v, r_up)])

    def emit_stats(sub, b, pr, with_rms, x_waits, extra_mm_waits=()):
        sl = slice(TB * b, TB * (b + 1))
        xbf = xbf2_sb if sub == 1 else xbf_sb
        cwts = list(x_waits)
        if ST.get("prev_sq_mm") is not None:
            cwts.append((s_t, ST["prev_sq_mm"]))
        cst = P.op("vector", lambda E, sl=sl, xbf=xbf: E.tensor_copy(
            xbf[:, :, :], x_sb[:, :, sl]), s_v, waits=cwts)
        mm = None
        for c in range(DC):
            mm = P.op("tensor", lambda E, c=c, xbf=xbf: E.matmul(
                pS1[0:1, :], ones_col_bf[:, :], xbf[:, c, :],
                start=(c == 0), stop=(c == DC - 1)),
                s_t if c == DC - 1 else None,
                waits=([(s_v, cst)] + list(extra_mm_waits)) if c == 0 else ())
        sum_mm = mm
        m_t = P.op("vector", lambda E: E.tensor_single_scalar(
            r0[:, :], pS1[0:1, :], 1.0 / D, ALU.mult), s_v, waits=[(s_t, sum_mm)])
        sq_t = P.op("scalar", lambda E, xbf=xbf: E.activation(
            xbf[:, :, :], xbf[:, :, :], AF.Square), s_s, waits=[(s_t, sum_mm)])
        mm = None
        for c in range(DC):
            mm = P.op("tensor", lambda E, c=c, xbf=xbf: E.matmul(
                pS2[0:1, :], ones_col_bf[:, :], xbf[:, c, :],
                start=(c == 0), stop=(c == DC - 1)),
                s_t if c == DC - 1 else None,
                waits=[(s_s, sq_t)] if c == 0 else ())
        sq_mm = mm
        ST["prev_sq_mm"] = sq_mm
        m2_t = P.op("vector", lambda E: E.tensor_tensor(
            r2[:, :], r0[:, :], r0[:, :], ALU.mult), s_v, waits=[(s_v, m_t)])
        var_t = P.op("vector", lambda E: E.scalar_tensor_tensor(
            r1[:, :], pS2[0:1, :], 1.0 / D, r2[:, :], ALU.mult, ALU.subtract), s_v,
            waits=[(s_t, sq_mm), (s_v, m2_t)])
        q_t = P.op("vector", lambda E: E.tensor_single_scalar(
            r2[:, :], r1[:, :], 1e-5, ALU.add), s_v, waits=[(s_v, var_t)])
        sq1 = P.op("scalar", lambda E: E.activation(r2[:, :], r2[:, :], AF.Sqrt),
                   s_s, waits=[(s_v, q_t)])
        ab_guard = [(s_t, ST["last_ab"])] if ST.get("last_ab") else []
        if with_rms:
            sq2 = P.op("scalar", lambda E: E.activation(r3[:, :], r1[:, :], AF.Sqrt),
                       s_s)
            dn_t = P.op("vector", lambda E: E.scalar_tensor_tensor(
                r1[:, :], r2[:, :], 1e-8, r3[:, :], ALU.mult, ALU.add), s_v,
                waits=[(s_s, sq2)])
            a_t = P.op("vector", lambda E: E.reciprocal(a_row[:, :], r1[:, :]),
                       s_v, waits=[(s_v, dn_t)] + ab_guard)
        else:
            a_t = P.op("vector", lambda E: E.reciprocal(a_row[:, :], r2[:, :]),
                       s_v, waits=[(s_s, sq1)] + ab_guard)
        b_t = P.op("vector", lambda E: E.scalar_tensor_tensor(
            b_row[:, :], r0[:, :], -1.0, a_row[:, :], ALU.mult, ALU.mult), s_v,
            waits=[(s_v, a_t)])
        return {"rows": b_t, "m": m_t, "sq_mm": sq_mm}

    def emit_bcast_ab(pr, rows_tgt, xn_done_tgt, half=False):
        lhs = half_row_f if half else ones_row_f
        wts = [(s_v, rows_tgt)]
        if xn_done_tgt:
            wts.append((s_v, xn_done_tgt))
        P.op("tensor", lambda E, lhs=lhs: E.matmul(
            pBA[:, :], lhs[:, :], a_row[0:1, :], start=True, stop=True),
            s_t, waits=wts)
        tb = P.op("tensor", lambda E, lhs=lhs: E.matmul(
            pBB[:, :], lhs[:, :], b_row[0:1, :], start=True, stop=True), s_t)
        ST["last_ab"] = tb
        return tb

    def emit_xn(sub, b, bcast_tgt, x_waits):
        sl = slice(TB * b, TB * (b + 1))
        dst = xn2_sb if sub == 1 else xn_sb
        tmp = oflat if sub == 1 else t1_sb
        t = None
        for c in range(DC):
            t1t = P.op("vector", lambda E, c=c, sl=sl, tmp=tmp: E.tensor_tensor(
                tmp[:, :], x_sb[:, c, sl], pBA[:, :], ALU.mult), s_v,
                waits=([(s_t, bcast_tgt)] + list(x_waits)) if c == 0 else ())
            t = P.op("vector", lambda E, c=c, dst=dst, tmp=tmp: E.tensor_tensor(
                dst[:, c, :], tmp[:, :], pBB[:, :], ALU.add), s_v,
                waits=[(s_v, t1t)])
        return t

    pr = 0
    st0 = emit_stats(0, 0, pr, True, x_waits=[(s_x, x_tgt)],
                     extra_mm_waits=[(s_v, consts_done)])
    bc0 = emit_bcast_ab(pr, st0["rows"], None, half=True)
    last_xn = emit_xn(0, 0, bc0, x_waits=[(s_x, x_tgt)])

    g1 = dma("gpsimd", al_in[:, :], al_pack[:, :], waits=[(s_v, alred_done)])
    cc1 = P.op("gpsimd", lambda E: E.collective_compute(
        "AllReduce", ALU.add, replica_groups=[list(range(NCORES))],
        ins=[al_in.ap().opt()], outs=[al_out.ap().opt()]),
        s_cc, 1, waits=[(s_gq, g1)])
    g2 = dma("gpsimd", al_gath[:, :], al_out[:, :], waits=[(s_cc, cc1)])
    mm1 = P.op("tensor", lambda E: E.matmul(pS1[0:1, 0:2], sc_col[:, :],
                                            al_gath[:, :], start=True, stop=True),
               s_t, waits=[(s_gq, g2), (s_v, last_xn)])
    cp1 = P.op("scalar", lambda E: E.copy(th_src[:, :], pS1[0:1, 0:2]), s_s,
               waits=[(s_t, mm1)])
    mm2 = P.op("tensor", lambda E: E.matmul(pS2[:, 0:2], ones_row_f[:, :],
                                            th_src[0:1, :], start=True, stop=True),
               s_t, waits=[(s_s, cp1)])
    th_done = None
    for dst_, col_, sc_ in ((th_up_p, 0, 1.0), (th_up_n, 0, -1.0),
                            (th_dn_p, 1, 1.0), (th_dn_n, 1, -1.0)):
        th_done = P.op("vector", lambda E, d=dst_, c=col_, s=sc_: E.tensor_single_scalar(
            d[:, :], pS2[:, c:c + 1], s, ALU.mult), s_v, waits=[(s_t, mm2)])

    last_comb = {}
    slot_last = {}
    WSEMS = (s_w0, s_w1, s_w2, s_w3)
    PAIRS = ((tneg0, tpos0), (tneg1, tpos1), (tneg2, tpos2))
    tern_up_v = tern_up_g = None
    for m in range(128):
        up = m < 64
        mm_ = m if up else m - 64
        j = mm_ // 2
        half = mm_ % 2
        if up:
            src = wupT[128 * (j // 4):128 * (j // 4 + 1),
                       KIB * (j % 4) + TB * half:KIB * (j % 4) + TB * (half + 1)]
            dst8 = wup8
            slot = m % 4
            stage = wstage4
        else:
            src = wdnT[128 * j:128 * (j + 1), TB * half:TB * (half + 1)]
            dst8 = wdn8
            slot = 2 + (m % 2)
            stage = wstage4
        slot_sem = WSEMS[slot]
        wts = []
        prev = slot_last.get(slot)
        if prev is not None:
            wts.append(prev)
        if m < 4:
            wts.append((s_v, alred_done))
        ld = dma("sync", stage[:, slot, :], src, waits=wts, sem=slot_sem)
        eng = "gpsimd" if (m % 4 == 3) else "vector"
        csem = CSEM[eng]
        thp = th_up_p if up else th_dn_p
        thn = th_up_n if up else th_dn_n
        dst_sl = (dst8, j, TB * half, TB * (half + 1))
        tn, tp = PAIRS[m % 3]
        swts = [(slot_sem, ld), (s_v, th_done)]
        if last_comb.get(m % 3) is not None:
            swts.append(last_comb[m % 3])
        P.op("scalar", lambda E, slot=slot, thn=thn, tn=tn, stage=stage: E.activation(
            tn[:, :], stage[:, slot, :], AF.Sign, bias=thn[:, 0:1]),
            waits=swts)
        sg = P.op("scalar", lambda E, slot=slot, thp=thp, tp=tp, stage=stage:
                  E.activation(tp[:, :], stage[:, slot, :], AF.Sign,
                               bias=thp[:, 0:1]), s_s)
        tt = P.op(eng, lambda E, d=dst_sl, tn=tn, tp=tp: E.tensor_tensor(
            d[0][:, d[1], d[2]:d[3]], tn[:, :], tp[:, :], ALU.add), csem,
            waits=[(s_s, sg)])
        last_comb[m % 3] = (csem, tt)
        slot_last[slot] = (csem, tt)
        if up and mm_ == 63:
            tern_up_v = P.total(s_v)
            tern_up_g = P.total(s_g)
            tern_up_s = P.total(s_s)
    tern_all_v = P.total(s_v)
    tern_all_g = P.total(s_g)
    tern_all_s = P.total(s_s)

    ep_tgt = [[None] * DC for _ in range(3 * NB)]

    def emit_ffn_block(sub, b, gb, xn_tgt, next_stats):
        sl = slice(TB * b, TB * (b + 1))
        gelu_t = {}
        sq_t = {}
        hh_mm = {}
        upstop = {}
        cast_t = {}
        CAST_CYCLE = ("vector", "scalar", "vector", "scalar", "gpsimd")
        for fc in range(FC):
            eng = CAST_CYCLE[fc % 5]
            csem = CSEM[eng]
            wts = []
            if upstop.get(fc - 2) is not None:
                wts.append((s_t, upstop[fc - 2]))
            if sub == 0:
                wts += [(s_v, tern_up_v), (s_g, tern_up_g), (s_s, tern_up_s)]
            else:
                wts.append((s_sp, ST["w8_reload_up"]))
            if eng == "scalar":
                cast_t[fc] = (csem, P.op(eng, lambda E, fc=fc: E.activation(
                    castup[:, fc % 2, :, :],
                    wup8[:, (fc // 8)::4, 128 * (fc % 8):128 * (fc % 8 + 1)],
                    AF.Identity), csem, waits=wts))
            else:
                cast_t[fc] = (csem, P.op(eng, lambda E, fc=fc: E.tensor_copy(
                    castup[:, fc % 2, :, :],
                    wup8[:, (fc // 8)::4, 128 * (fc % 8):128 * (fc % 8 + 1)]),
                    csem, waits=wts))
            w0 = [cast_t[fc], (s_v, xn_tgt)]
            if gelu_t.get(fc - 2) is not None:
                w0.append((s_s, gelu_t[fc - 2]))
            mm = None
            for dc in range(DC):
                mm = P.op("tensor", lambda E, fc=fc, dc=dc: E.matmul(
                    pUP[fc % 2][:, :], castup[:, fc % 2, dc, :], xn_sb[:, dc, :],
                    start=(dc == 0), stop=(dc == DC - 1)),
                    s_t if dc == DC - 1 else None, waits=w0 if dc == 0 else ())
            upstop[fc] = mm
            gelu_t[fc] = P.op("scalar", lambda E, fc=fc: E.activation(
                h_sb[:, fc, :], pUP[fc % 2][:, :], AF.Gelu), s_s,
                waits=[(s_t, upstop[fc])])
            if fc >= 2:
                f2 = fc - 2
                hh_mm[f2] = P.op("tensor", lambda E, f2=f2: E.matmul(
                    pS1[0:1, :], ones_col_bf[:, :], hsq_sb[:, f2 % 2, :],
                    start=(f2 == 0), stop=(f2 == FC - 1)),
                    s_t, waits=[(s_s, sq_t[f2])])
            sqw = [(s_s, gelu_t[fc])]
            if hh_mm.get(fc - 2) is not None:
                sqw.append((s_t, hh_mm[fc - 2]))
            if sub == 0 and b == 0 and fc < 2:
                sqw += [(s_v, tern_all_v), (s_g, tern_all_g)]
            sq_t[fc] = P.op("scalar", lambda E, fc=fc: E.activation(
                hsq_sb[:, fc % 2, :], h_sb[:, fc, :], AF.Square), s_s, waits=sqw)
        for f2 in (FC - 2, FC - 1):
            hh_mm[f2] = P.op("tensor", lambda E, f2=f2: E.matmul(
                pS1[0:1, :], ones_col_bf[:, :], hsq_sb[:, f2 % 2, :],
                start=(f2 == 0), stop=(f2 == FC - 1)),
                s_t, waits=[(s_s, sq_t[f2])])
        hh_done = hh_mm[FC - 1]
        cpre = P.op("vector", lambda E: E.tensor_single_scalar(
            r3[:, :], pS1[0:1, :], 1.0 / DF, ALU.mult), s_v,
            waits=[(s_t, hh_done)])
        csq = P.op("scalar", lambda E: E.activation(r3[:, :], r3[:, :], AF.Sqrt),
                   s_s, waits=[(s_v, cpre)])
        ct1 = P.op("vector", lambda E: E.tensor_single_scalar(
            r3[:, :], r3[:, :], 1e-8, ALU.add), s_v, waits=[(s_s, csq)])
        cb_guard = [(s_t, ST["last_cb"])] if ST.get("last_cb") else []
        ct2 = P.op("vector", lambda E: E.reciprocal(c_row[:, :], r3[:, :]), s_v,
                   waits=cb_guard + [(s_v, ct1)])
        st_info = None
        if next_stats is not None:
            st_info = next_stats(extra_mm_waits=[(s_s, csq)])
        cwts = [(s_v, ct2)]
        if st_info is not None:
            cwts.append((s_v, st_info["m"]))
        cb = P.op("tensor", lambda E: E.matmul(pS1[:, :], quar_row_f[:, :],
                                               c_row[0:1, :], start=True, stop=True),
                  s_t, waits=cwts)
        ST["last_cb"] = cb
        cwts2 = [(s_t, cb)]
        if sub == 0 and b == 0:
            cwts2 += [(s_v, tern_all_v), (s_g, tern_all_g)]
        csb = P.op("scalar", lambda E: E.copy(csb_sb[:, :], pS1[:, :]), s_s,
                   waits=cwts2)
        dnstop = {}
        castd_t = {}
        qend = {}
        ep_list = []
        CAST_CYCLE_D = ("scalar", "vector", "gpsimd", "vector", "scalar")

        def emit_castd(dc, q):
            key = 4 * dc + q
            eng = CAST_CYCLE_D[key % 5]
            csem = CSEM[eng]
            wts = []
            if qend.get(key - 2) is not None:
                wts.append((s_t, qend[key - 2]))
            if sub == 0 and b == 0 and key < 2:
                wts += [(s_v, tern_all_v), (s_g, tern_all_g), (s_s, tern_all_s)]
            if sub == 2 and key < 2:
                wts.append((s_sp, ST["w8_reload_dn"]))
            if eng == "scalar":
                castd_t[key] = (csem, P.op(eng, lambda E, q=q, dc=dc, key=key:
                                           E.activation(
                    castdn[:, key % 2, :, :],
                    wdn8[:, 8 * q:8 * (q + 1), 128 * dc:128 * (dc + 1)],
                    AF.Identity), csem, waits=wts))
            else:
                castd_t[key] = (csem, P.op(eng, lambda E, q=q, dc=dc, key=key:
                                           E.tensor_copy(
                    castdn[:, key % 2, :, :],
                    wdn8[:, 8 * q:8 * (q + 1), 128 * dc:128 * (dc + 1)]),
                    csem, waits=wts))

        for dc in range(DC):
            emit_castd(dc, 0)
            emit_castd(dc, 1)
            w0 = [(s_s, gelu_t[FC - 1])]
            if dc >= 2 and ep_tgt[gb][dc - 2] is not None:
                w0.append((s_v, ep_tgt[gb][dc - 2]))
            mm = None
            for fc in range(FC):
                if fc == 8:
                    emit_castd(dc, 2)
                if fc == 16:
                    emit_castd(dc, 3)
                fw = list(w0) if fc == 0 else []
                if fc % 8 == 0:
                    fw.append(castd_t[4 * dc + fc // 8])
                mm = P.op("tensor", lambda E, dc=dc, fc=fc: E.matmul(
                    pDN[dc % 2][:, :], castdn[:, (4 * dc + fc // 8) % 2, fc % 8, :],
                    h_sb[:, fc, :], start=(fc == 0), stop=(fc == FC - 1)),
                    s_t if fc % 8 == 7 else None, waits=fw)
                if fc % 8 == 7:
                    qend[4 * dc + fc // 8] = mm
            dnstop[dc] = mm
            e1 = P.op("vector", lambda E, dc=dc: E.tensor_tensor(
                t1_sb[:, :], pDN[dc % 2][:, :], csb_sb[:, :], ALU.mult), s_v,
                waits=[(s_t, dnstop[dc]), (s_s, csb)])
            ep = P.op("vector", lambda E, dc=dc, sl=sl: E.tensor_tensor(
                x_sb[:, dc, sl], x_sb[:, dc, sl], t1_sb[:, :], ALU.add), s_v,
                waits=[(s_v, e1)])
            ep_tgt[gb][dc] = ep
            ep_list.append(ep)
        return ep_list, st_info

    def emit_ssm_block(b, gb, xn_tgt, next_stats, prev_guard):
        sl = slice(TB * b, TB * (b + 1))
        pc = b % 2
        cpc_o, cpc_n = cpc_pp[pc], cpc_pp[1 - pc]
        stc_o, stc_n = stc_pp[pc], stc_pp[1 - pc]
        evd = {}
        evi = {}
        decstop = {}
        injstop = {}
        for ec in range(DC):
            wts = [(s_gq, ST["wdec_ld"])] if ec == 0 else []
            if ec == 0:
                wts.append((s_v, xn_tgt))
            if evd.get(ec - 2) is not None:
                wts.append((s_s, evd[ec - 2]))
            mm = None
            for dc in range(DC):
                mm = P.op("tensor", lambda E, ec=ec, dc=dc: E.matmul(
                    pUP[ec % 2][:, :], wdec_sb[:, dc, 128 * ec:128 * (ec + 1)],
                    xn2_sb[:, dc, :], start=(dc == 0), stop=(dc == DC - 1)),
                    s_t if dc == DC - 1 else None, waits=wts if dc == 0 else ())
            decstop[ec] = mm
            ewts = [(s_t, decstop[ec]), (s_aq, ST["bias_ld"])]
            if prev_guard is not None:
                ewts.append((s_v, prev_guard["dec"][ec]))
            evd[ec] = P.op("scalar", lambda E, ec=ec: E.activation(
                dec_sb[:, ec, :], pUP[ec % 2][:, :], AF.Sigmoid,
                bias=bdec_sb[:, ec:ec + 1]), s_s, waits=ewts)
        for ec in range(DC):
            wts = [(s_gq, ST["winp_ld"])] if ec == 0 else []
            if evi.get(ec - 2) is not None:
                wts.append((s_s, evi[ec - 2]))
            mm = None
            for dc in range(DC):
                mm = P.op("tensor", lambda E, ec=ec, dc=dc: E.matmul(
                    pDN[ec % 2][:, :], winp_sb[:, dc, 128 * ec:128 * (ec + 1)],
                    xn2_sb[:, dc, :], start=(dc == 0), stop=(dc == DC - 1)),
                    s_t if dc == DC - 1 else None, waits=wts if dc == 0 else ())
            injstop[ec] = mm
            ewts = [(s_t, injstop[ec]), (s_aq, ST["bias_ld"])]
            if prev_guard is not None:
                ewts.append((s_v, prev_guard["inj"][ec]))
            evi[ec] = P.op("scalar", lambda E, ec=ec: E.activation(
                inj_sb[:, ec, :], pDN[ec % 2][:, :], AF.Identity,
                bias=binp_sb[:, ec:ec + 1]), s_s, waits=ewts)
        cp_t = {}
        for ec in range(DC):
            cp_t[ec] = P.op("vector", lambda E, ec=ec: E.tensor_tensor_scan(
                cp32_sb[:, ec, :], dec_sb[:, ec, :], zeros_sb[:, :], 1.0,
                ALU.mult, ALU.add), s_v, waits=[(s_s, evd[ec])])
        cpf_w = list(ST["cpf_guard"]) if b == 0 else []
        cpf = None
        for ec in range(DC):
            cpf = P.op("vector", lambda E, ec=ec, sl=sl: E.tensor_single_scalar(
                cp_full[:, ec, sl], cp32_sb[:, ec, :], cpc_o[:, ec:ec + 1],
                ALU.mult), s_v,
                waits=(cpf_w if ec == 0 else []) + [(s_v, cp_t[ec])])
        cpn = P.op("vector", lambda E: E.tensor_tensor(
            cpc_n[:, :], cp32_sb[:, :, TB - 1:TB], cpc_o[:, :], ALU.mult), s_v,
            waits=[(s_v, cpf)])
        spacer("vector")
        dec_guard = {}
        inj_guard = {}
        ep = None
        for ec in range(DC):
            inj_guard[ec] = P.op("vector", lambda E, ec=ec: E.tensor_copy(
                iflat[:, :], inj_sb[:, ec, :]), s_v, waits=[(s_s, evi[ec])])
            dec_guard[ec] = P.op("vector", lambda E, ec=ec: E.tensor_tensor_scan(
                oflat[:, :], dec_sb[:, ec, :], iflat[:, :], 0.0,
                ALU.mult, ALU.add), s_v, waits=[(s_v, inj_guard[ec])])
            sfx = P.op("vector", lambda E, ec=ec: E.scalar_tensor_tensor(
                iflat[:, :], cp32_sb[:, ec, :], stc_o[:, ec:ec + 1], oflat[:, :],
                ALU.mult, ALU.add), s_v, waits=[(s_v, dec_guard[ec])])
            ep = P.op("vector", lambda E, ec=ec, sl=sl: E.tensor_tensor(
                x_sb[:, ec, sl], x_sb[:, ec, sl], iflat[:, :], ALU.add), s_v,
                waits=[(s_v, sfx)])
            P.op("vector", lambda E, ec=ec: E.tensor_copy(
                stc_n[:, ec:ec + 1], iflat[:, TB - 1:TB]), s_v, waits=[(s_v, ep)])
        st_info = next_stats(extra_mm_waits=()) if next_stats is not None else None
        for dc in range(DC):
            ep_tgt[gb][dc] = ep
        return {"ep": ep, "cpf": cpf, "cpn": cpn, "st_info": st_info,
                "guard": {"dec": dec_guard, "inj": inj_guard},
                "carry_new": 1 - pc}

    NBF = debug_nb if sublayers == 1 else NB
    for sub in [0, 1, 2][:sublayers]:
        if sub in (0, 2):
            for b in range(NBF):
                gb = (0 if sub == 0 else 2 * NB) + b
                npr = 1 - pr
                nxt = None
                if b < NBF - 1:
                    if sub == 0:
                        nxt = (lambda extra_mm_waits, b=b, npr=npr: emit_stats(
                            0, b + 1, npr, True, x_waits=[(s_x, x_tgt)],
                            extra_mm_waits=extra_mm_waits))
                    else:
                        nxt = (lambda extra_mm_waits, b=b, npr=npr: emit_stats(
                            2, b + 1, npr, True, x_waits=[(s_v, ST["fix_done"])],
                            extra_mm_waits=extra_mm_waits))

                eps, st_info = emit_ffn_block(sub, b, gb, last_xn, nxt)
                if st_info is not None:
                    nsub = sub if b < NB - 1 else 1
                    nb = b + 1 if b < NB - 1 else 0
                    bc = emit_bcast_ab(npr, st_info["rows"], last_xn, half=(nsub != 1))
                    last_xn = emit_xn(nsub, nb, bc, x_waits=[(s_v, eps[-1])])
                    pr = npr
                if sub == 2:
                    yr = yT.rearrange("(c p) t -> p c t", p=128)
                    sl = slice(TB * b, TB * (b + 1))
                    ST["store"] = dma("scalar", yr[:, :, sl], x_sb[:, :, sl],
                                      waits=[(s_v, eps[-1])])
        else:
            spill_up = dma("sync", sp_up[:, :, :], wup8[:, :, :],
                           waits=[(s_v, tern_all_v), (s_g, tern_all_g)], sem=s_sp)
            wdec_ld = dma("gpsimd", wdec_sb[:, :, :],
                          wdecT.rearrange("(c p) e -> p c e", p=128),
                          waits=[(s_sp, spill_up), (s_v, P.total(s_v)),
                                 (s_g, P.total(s_g))])
            winp_ld = dma("gpsimd", winp_sb[:, :, :],
                          winpT.rearrange("(c p) e -> p c e", p=128))
            ST["wdec_ld"] = winp_ld
            ST["winp_ld"] = winp_ld
            spill_dn = dma("sync", sp_dn[:, :, :], wdn8[:, :, :],
                           waits=[(s_sp, spill_up), (s_v, P.total(s_v)),
                                  (s_g, P.total(s_g))], sem=s_sp)
            ST["cpf_guard"] = [(s_sp, spill_dn)]
            npr = 1 - pr
            st1 = emit_stats(1, 0, npr, False,
                             x_waits=[(s_v, ep_tgt[NB - 1][DC - 1])])
            bc1 = emit_bcast_ab(npr, st1["rows"], last_xn, half=False)
            last_xn = emit_xn(1, 0, bc1, x_waits=[(s_v, ep_tgt[NB - 1][DC - 1])])
            pr = npr
            res = None
            prev_guard = None
            for b in range(NB):
                gb = NB + b
                npr = 1 - pr
                nxt = None
                if b < NB - 1:
                    nxt = (lambda extra_mm_waits, b=b, npr=npr: emit_stats(
                        1, b + 1, npr, False,
                        x_waits=[(s_v, ep_tgt[b + 1][DC - 1])],
                        extra_mm_waits=extra_mm_waits))
                res = emit_ssm_block(b, gb, last_xn, nxt, prev_guard)
                prev_guard = res["guard"]
                if res["st_info"] is not None:
                    bc = emit_bcast_ab(npr, res["st_info"]["rows"], last_xn, half=False)
                    last_xn = emit_xn(1, b + 1, bc, x_waits=[(s_v, res["ep"])])
                    pr = npr
            cnew = res["carry_new"]
            pk1 = P.op("vector", lambda E: E.tensor_copy(
                exch_sb[:, 0:8], cpc_pp[cnew][:, :]), s_v, waits=[(s_v, res["cpn"])])
            pk2 = P.op("vector", lambda E: E.tensor_copy(
                exch_sb[:, 8:16], stc_pp[cnew][:, :]), s_v, waits=[(s_v, res["ep"])])
            ge = dma("gpsimd", ag_in[:, :], exch_sb[:, :], waits=[(s_v, pk2)])
            cct = P.op("gpsimd", lambda E: E.collective_compute(
                "AllGather", ALU.bypass, replica_groups=[list(range(NCORES))],
                ins=[ag_in.ap().opt()], outs=[ag_out.ap().opt()]),
                s_cc, 1, waits=[(s_gq, ge)])
            gg = dma("gpsimd", gath_sb[:, :, :],
                     ag_out.ap().rearrange("j p k -> p j k"), waits=[(s_cc, cct)])
            pt = P.op("vector", lambda E: E.tensor_copy(S_pp[0][:, :],
                                                        gath_sb[:, 0, 8:16]), s_v,
                      waits=[(s_gq, gg), (s_aq, misc_tgt)])
            spacer("vector")
            P.op("vector", lambda E: E.tensor_copy(cs_sb[:, :, 0], S_pp[0][:, :]),
                 s_v, waits=[(s_v, pt)])
            sj = 0
            for j in range(1, 8):
                spacer("vector")
                pt = P.op("vector", lambda E, j=j, sj=sj: E.tensor_tensor(
                    tmpS[:, :], gath_sb[:, j, 0:8], S_pp[sj][:, :], ALU.mult), s_v,
                    waits=[(s_v, pt)])
                spacer("vector")
                pt = P.op("vector", lambda E, j=j, sj=sj: E.tensor_tensor(
                    S_pp[1 - sj][:, :], tmpS[:, :], gath_sb[:, j, 8:16], ALU.add),
                    s_v, waits=[(s_v, pt)])
                sj = 1 - sj
                spacer("vector")
                pt = P.op("vector", lambda E, j=j, sj=sj: E.tensor_copy(
                    cs_sb[:, :, j], S_pp[sj][:, :]), s_v, waits=[(s_v, pt)])
            spacer("vector")
            ml = P.op("vector", lambda E: E.tensor_tensor(
                tmp88_sb[:, :, :], cs_sb[:, :, :], selm_sb[:, :, :], ALU.mult), s_v,
                waits=[(s_v, pt)])
            spacer("vector")
            rd = None
            for c in range(DC):
                rd = P.op("vector", lambda E, c=c: E.tensor_reduce(
                    sin_sb[:, c:c + 1], tmp88_sb[:, c, :], AX.X, ALU.add), s_v,
                    waits=[(s_v, ml)] if c == 0 else ())
            spacer("vector")
            fx = None
            for c in range(DC):
                fx = P.op("vector", lambda E, c=c: E.scalar_tensor_tensor(
                    x_sb[:, c, :], cp_full[:, c, :], sin_sb[:, c:c + 1],
                    x_sb[:, c, :], ALU.mult, ALU.add), s_v,
                    waits=[(s_v, rd), (s_v, res["cpf"])] if c == 0 else ())
            ST["fix_done"] = fx
            if sublayers > 2:
                rl_up = dma("sync", wup8[:, :, :], sp_up[:, :, :],
                            waits=[(s_t, P.total(s_t)), (s_gq, P.total(s_gq)),
                                   (s_sp, spill_dn)], sem=s_sp)
                rl_dn = dma("sync", wdn8[:, :, :], sp_dn[:, :, :],
                            waits=[(s_v, fx), (s_sp, rl_up)], sem=s_sp)
                ST["w8_reload_up"] = rl_up
                ST["w8_reload_dn"] = rl_dn
                npr = 1 - pr
                st2 = emit_stats(2, 0, npr, True, x_waits=[(s_v, fx)])
                bc2 = emit_bcast_ab(npr, st2["rows"], last_xn, half=True)
                last_xn = emit_xn(2, 0, bc2, x_waits=[(s_v, fx)])
                pr = npr

    if sublayers < 3:
        yr = yT.rearrange("(c p) t -> p c t", p=128)
        dma("scalar", yr[:, :, :], x_sb[:, :, :],
            waits=[(s_v, P.total(s_v)), (s_s, P.total(s_s)), (s_g, P.total(s_g))])
    if sublayers == 2:
        for nm, t_, shp in (
            ("s_dec", dec_sb, [128, DC, TB]), ("s_inj", inj_sb, [128, DC, TB]),
            ("s_cp32", cp32_sb, [128, DC, TB]), ("s_sin", sin_sb, [128, DC]),
            ("s_gath", gath_sb, [128, 8, 16]), ("s_exch", exch_sb, [128, 16]),
            ("s_cpc", cpc_pp[0], [128, DC]), ("s_stc", stc_pp[0], [128, DC])):
            dr = nc.dram_tensor(nm, shp, F32, kind="ExternalOutput").ap()
            dma("scalar", dr, t_[tuple(slice(None) for _ in shp)],
                waits=[(s_v, P.total(s_v)), (s_s, P.total(s_s)),
                       (s_g, P.total(s_g)), (s_t, P.total(s_t))])
        drc = nc.dram_tensor("s_cpf", [128, DC, TC], BF16, kind="ExternalOutput").ap()
        dma("scalar", drc, cp_full[:, :, :],
            waits=[(s_v, P.total(s_v))])
        drx = nc.dram_tensor("s_xn2", [128, DC, TB], BF16, kind="ExternalOutput").ap()
        dma("scalar", drx, xn2_sb[:, :, :], waits=[(s_v, P.total(s_v))])
    if sublayers == 1 and debug_nb == 1:
        dbg = {}
        for nm, t_, shp, dt_ in (
            ("d_wup8", wup8, [128, FC, KIB], F8), ("d_wdn8", wdn8, [128, FC, KIB], F8),
            ("d_xn", xn_sb, [128, DC, TB], BF16), ("d_h", h_sb, [128, FC, TB], BF16),
            ("d_csb", csb_sb, [128, TB], F32), ("d_arow", a_row, [1, TB], F32),
            ("d_brow", b_row, [1, TB], F32), ("d_crow", c_row, [1, TB], F32),
            ("d_thup", th_up_p, [128, 1], F32), ("d_thdn", th_dn_p, [128, 1], F32),
            ("d_alg", al_gath, [128, 2], F32), ("d_alp", al_pack, [128, 2], F32)):
            dr = nc.dram_tensor(nm, shp, dt_, kind="ExternalOutput").ap()
            dma("scalar", dr, t_[tuple(slice(None) for _ in shp)],
                waits=[(s_v, P.total(s_v)), (s_s, P.total(s_s)),
                       (s_g, P.total(s_g)), (s_t, P.total(s_t))])

    P.op("sync", lambda E: E.nop(), waits=[(s_aq, P.total(s_aq)),
                                           (s_dq, P.total(s_dq)),
                                           (s_gq, P.total(s_gq)),
                                           (s_x, P.total(s_x)),
                                           (s_w0, P.total(s_w0)),
                                           (s_w1, P.total(s_w1)),
                                           (s_w2, P.total(s_w2)),
                                           (s_w3, P.total(s_w3)),
                                           (s_sp, P.total(s_sp))])

    block = es.enter_context(nc.Block())

    def runner(oplist):
        def f(E):
            for fn in oplist:
                fn(E)
        return f

    block.tensor(runner(P.ops["tensor"]))
    block.scalar(runner(P.ops["scalar"]))
    block.vector(runner(P.ops["vector"]))
    block.gpsimd(runner(P.ops["gpsimd"]))
    block.sync(runner(P.ops["sync"]))
    es.close()
    return nc


def kernel(L0_attn_x_seq, L0_attn_W_decay, L0_attn_b_decay, L0_attn_W_input,
           L0_attn_b_input, L0_ffn_w_up, L0_ffn_w_down, sublayers=3, debug_nb=NB,
           _dbg_out=None):
    x = np.asarray(L0_attn_x_seq)[0]
    xT = np.ascontiguousarray(x.T.astype(np.float32))
    wupT = np.ascontiguousarray(np.asarray(L0_ffn_w_up).T.astype(np.float32))
    wdnT = np.ascontiguousarray(np.asarray(L0_ffn_w_down).T.astype(np.float32))
    wdecT = np.ascontiguousarray(np.asarray(L0_attn_W_decay).T.astype(np.float32))
    winpT = np.ascontiguousarray(np.asarray(L0_attn_W_input).T.astype(np.float32))
    bdec = np.ascontiguousarray(
        np.asarray(L0_attn_b_decay).reshape(DC, 128).T.astype(np.float32))
    binp = np.ascontiguousarray(
        np.asarray(L0_attn_b_input).reshape(DC, 128).T.astype(np.float32))

    nc = build_nc(sublayers=sublayers, debug_nb=debug_nb)
    in_maps = []
    for i in range(NCORES):
        sel = np.zeros((128, DC, 8), np.float32)
        if i > 0:
            sel[:, :, i - 1] = 1.0
        in_maps.append({
            "xT": np.ascontiguousarray(xT[:, i * TC:(i + 1) * TC]),
            "wupT": wupT, "wdnT": wdnT, "wdecT": wdecT, "winpT": winpT,
            "wup_sl": np.ascontiguousarray(wupT[128 * i:128 * (i + 1), :]),
            "wdn_sl": np.ascontiguousarray(wdnT[512 * i:512 * (i + 1), :]),
            "bdec": bdec, "binp": binp, "selm": sel,
        })
    res = run_bass_kernel_spmd(nc, in_maps, core_ids=list(range(NCORES)))
    if _dbg_out is not None:
        for r in res.results:
            _dbg_out.append({k_: np.asarray(v_) for k_, v_ in r.items()})
    yT_full = np.concatenate([r["yT"] for r in res.results], axis=1)
    return np.ascontiguousarray(yT_full.T)[None].astype(np.float32)


# version 06 (final)
# speedup vs baseline: 1.0766x; 1.0346x over previous
import numpy as np
from contextlib import ExitStack

import concourse.bass as bass
import concourse.mybir as mybir
from concourse.bass_utils import run_bass_kernel_spmd

F32 = mybir.dt.float32
BF16 = mybir.dt.bfloat16
F8 = mybir.dt.float8e4
U8 = mybir.dt.uint8
AF = mybir.ActivationFunctionType
ALU = mybir.AluOpType
AX = mybir.AxisListType

NCORES = 8
D = 1024
DC = 8
DF = 4096
FC = 32
TC = 2048
NB = 4
TB = 512
KIB = 1024

ENGINES = ("tensor", "scalar", "vector", "gpsimd", "sync")


class Planner:
    def __init__(self):
        self.ops = {e: [] for e in ENGINES}
        self.cnt = {}
        self.lastw = {e: {} for e in ENGINES}
        self.vsem = None

    def total(self, sem):
        return self.cnt.get(sem.name, 0)

    def wait(self, eng, sem, val):
        if sem is None or val is None or val <= 0:
            return
        if self.lastw[eng].get(sem.name, 0) >= val:
            return
        self.lastw[eng][sem.name] = val
        self.ops[eng].append(lambda E, s=sem, v=val: E.wait_ge(s, v))

    def op(self, eng, build, sem=None, n=1, waits=()):
        if eng == "vector" and self.vsem is not None:
            if sem is None:
                sem = self.vsem
            if sem is self.vsem:
                prev = self.cnt.get(sem.name, 0)
                if prev > 0:
                    self.wait(eng, sem, prev)
        for s, v in waits:
            self.wait(eng, s, v)
        if sem is None:
            self.ops[eng].append(build)
            return None
        self.cnt[sem.name] = self.cnt.get(sem.name, 0) + n
        tgt = self.cnt[sem.name]
        self.ops[eng].append(lambda E, b=build, s=sem, k=n: b(E).then_inc(s, k))
        return tgt


def build_nc(sublayers=3, debug_nb=NB):
    nc = bass.Bass("TRN2", target_bir_lowering=False, debug=False)

    xT = nc.dram_tensor("xT", [D, TC], F32, kind="ExternalInput").ap()
    wupT = nc.dram_tensor("wupT", [D, DF], F32, kind="ExternalInput").ap()
    wdnT = nc.dram_tensor("wdnT", [DF, D], F32, kind="ExternalInput").ap()
    wdecT = nc.dram_tensor("wdecT", [D, D], F32, kind="ExternalInput").ap()
    winpT = nc.dram_tensor("winpT", [D, D], F32, kind="ExternalInput").ap()
    wup_sl = nc.dram_tensor("wup_sl", [128, DF], F32, kind="ExternalInput").ap()
    wdn_sl = nc.dram_tensor("wdn_sl", [512, D], F32, kind="ExternalInput").ap()
    bdec = nc.dram_tensor("bdec", [128, DC], F32, kind="ExternalInput").ap()
    binp = nc.dram_tensor("binp", [128, DC], F32, kind="ExternalInput").ap()
    selm = nc.dram_tensor("selm", [128, DC, 8], F32, kind="ExternalInput").ap()
    yT = nc.dram_tensor("yT", [D, TC], F32, kind="ExternalOutput").ap()
    al_in = nc.dram_tensor("al_in", [128, 2], F32)
    al_out = nc.dram_tensor("al_out", [128, 2], F32)
    ag_in = nc.dram_tensor("ag_in", [128, 16], F32)
    ag_out = nc.dram_tensor("ag_out", [8, 128, 16], F32)
    sp_up = nc.dram_tensor("sp_up", [128, FC, KIB], F8)
    sp_dn = nc.dram_tensor("sp_dn", [128, FC, KIB], F8)

    es = ExitStack()

    def sb(name, shape, dtype):
        return es.enter_context(nc.sbuf_tensor(name, shape, dtype))

    x_sb = sb("x_sb", [128, DC, TC], F32)
    ones_col_bf = sb("ones_col_bf", [128, 1], BF16)
    ones_row_f = sb("ones_row_f", [1, 128], F32)
    half_row_f = sb("half_row_f", [1, 128], F32)
    quar_row_f = sb("quar_row_f", [1, 128], F32)
    bdec_sb = sb("bdec_sb", [128, DC], F32)
    binp_sb = sb("binp_sb", [128, DC], F32)
    selm_sb = sb("selm_sb", [128, DC, 8], F32)
    exch_sb = sb("exch_sb", [128, 16], F32)
    gath_sb = sb("gath_sb", [128, 8, 16], F32)
    cs_sb = sb("cs_sb", [128, DC, 8], F32)
    tmp88_sb = sb("tmp88_sb", [128, DC, 8], F32)
    sin_sb = sb("sin_sb", [128, DC], F32)
    cpc_pp = [sb("cpc_a", [128, DC], F32), sb("cpc_b", [128, DC], F32)]
    stc_pp = [sb("stc_a", [128, DC], F32), sb("stc_b", [128, DC], F32)]
    iflat = sb("iflat", [128, TB], F32)
    oflat = sb("oflat", [128, TB], F32)
    S_pp = [sb("S_a", [128, DC], F32), sb("S_b", [128, DC], F32)]
    tmpS = sb("tmpS", [128, DC], F32)
    al_pack = sb("al_pack", [128, 2], F32)
    acc16 = sb("acc16", [128, 16], F32)
    zeros_sb = sb("zeros_sb", [128, TB], F32)
    sc_col = sb("sc_col", [128, 1], F32)
    spc_sb = sb("spc_sb", [1, 8], F32)
    al_gath = sb("al_gath", [128, 2], F32)
    th_src = sb("th_src", [1, 2], F32)
    th_up_p = sb("th_up_p", [128, 1], F32)
    th_up_n = sb("th_up_n", [128, 1], F32)
    th_dn_p = sb("th_dn_p", [128, 1], F32)
    th_dn_n = sb("th_dn_n", [128, 1], F32)
    a_row = sb("a_row", [1, TB], F32)
    b_row = sb("b_row", [1, TB], F32)
    c_row = sb("c_row", [1, TB], F32)
    r0 = sb("r0", [1, TB], F32)
    r1 = sb("r1", [1, TB], F32)
    r2 = sb("r2", [1, TB], F32)
    r3 = sb("r3", [1, TB], F32)

    ARENA = 120 * KIB
    arena = es.enter_context(nc.sbuf_tensor("arena", [128, ARENA], U8))
    arena_base = nc.lookup_mloc(arena).addr

    def at(off_kib, name, shape, dtype):
        hnd = nc.virtual_sbuf_tensor(name, shape, dtype)
        nc._tensor(hnd.name, hnd.shape, hnd.dtype, type="SB")
        mloc = nc.lookup_mloc(hnd)
        mloc.allocated = True
        mloc.addr = arena_base + int(off_kib * KIB)
        return hnd

    wup8 = at(0, "wup8", [128, FC, KIB], F8)
    wdn8 = at(32, "wdn8", [128, FC, KIB], F8)
    h_sb = at(64, "h_sb", [128, FC, TB], BF16)
    xn_sb = at(96, "xn_sb", [128, DC, TB], BF16)
    xbf_sb = at(96, "xbf_sb", [128, DC, TB], BF16)
    castup = at(104, "castup", [128, 2, DC, 128], BF16)
    castdn = at(108, "castdn", [128, 2, DC, 128], BF16)
    t1_sb = at(112, "t1_sb", [128, TB], F32)
    csb_sb = at(114, "csb_sb", [128, TB], F32)
    hsq_sb = at(116, "hsq_sb", [128, 2, TB], BF16)
    wstage = at(108, "wstage", [128, 2, TB], F32)
    wstage4 = at(104, "wstage4", [128, 4, TB], F32)
    tneg2 = at(114, "tneg2", [128, TB], BF16)
    tpos2 = at(115, "tpos2", [128, TB], BF16)
    tneg0 = at(116, "tneg0", [128, TB], BF16)
    tpos0 = at(117, "tpos0", [128, TB], BF16)
    tneg1 = at(118, "tneg1", [128, TB], BF16)
    tpos1 = at(119, "tpos1", [128, TB], BF16)
    wdec_sb = at(0, "wdec_sb", [128, DC, KIB], BF16)
    winp_sb = at(16, "winp_sb", [128, DC, KIB], BF16)
    cp_full = at(32, "cp_full", [128, DC, TC], BF16)
    dec_sb = at(64, "dec_sb", [128, DC, TB], F32)
    inj_sb = at(80, "inj_sb", [128, DC, TB], F32)
    cp32_sb = at(96, "cp32_sb", [128, DC, TB], F32)
    xn2_sb = at(112, "xn2_sb", [128, DC, TB], BF16)
    xbf2_sb = at(112, "xbf2_sb", [128, DC, TB], BF16)

    def psum(name):
        return es.enter_context(nc.psum_tensor(name, [128, TB], F32))

    pUP = [psum("pUP0"), psum("pUP1")]
    pDN = [psum("pDN0"), psum("pDN1")]
    pS1 = psum("pS1")
    pS2 = psum("pS2")
    pBA = psum("pBA")
    pBB = psum("pBB")

    def sem(name):
        return es.enter_context(nc.semaphore(name))

    s_dq = sem("s_dq")
    s_aq = sem("s_aq")
    s_gq = sem("s_gq")
    s_x = sem("s_x")
    s_w0 = sem("s_w0")
    s_w1 = sem("s_w1")
    s_w2 = sem("s_w2")
    s_w3 = sem("s_w3")
    s_sp = sem("s_sp")
    s_t = sem("s_t")
    s_s = sem("s_s")
    s_v = sem("s_v")
    s_g = sem("s_g")
    s_cc = sem("s_cc")

    P = Planner()
    P.vsem = s_v
    DMAS = {"sync": s_dq, "scalar": s_aq, "gpsimd": s_gq}
    CSEM = {"vector": s_v, "gpsimd": s_g, "scalar": s_s, "tensor": s_t}
    ST = {}

    def spacer(eng):
        P.op(eng, lambda E: E.tensor_single_scalar(spc_sb[:, :], spc_sb[:, :], 0.0,
                                                   ALU.add))

    def dma(eng, out_ap, in_ap, waits=(), sem=None):
        return P.op(eng, lambda E, o=out_ap, i=in_ap: E.dma_start(out=o, in_=i),
                    sem if sem is not None else DMAS[eng], 16, waits)

    xr = xT.rearrange("(c p) t -> p c t", p=128)
    x_tgt = None
    for i in range(4):
        x_tgt = dma("scalar", x_sb[:, 2 * i:2 * i + 2, :], xr[:, 2 * i:2 * i + 2, :],
                    sem=s_x)
    dma("scalar", bdec_sb[:, :], bdec)
    binp_tgt = dma("scalar", binp_sb[:, :], binp)
    misc_tgt = dma("scalar", selm_sb[:, :, :], selm)
    ST["x"] = x_tgt
    ST["bias_ld"] = misc_tgt

    for t_, v_ in ((ones_col_bf, 1.0), (ones_row_f, 1.0), (half_row_f, 0.5),
                   (quar_row_f, 0.25),
                   (al_pack, 0.0), (zeros_sb, 0.0), (cpc_pp[0], 1.0),
                   (stc_pp[0], 0.0), (sc_col, 0.5 / (D * DF)), (spc_sb, 0.0)):
        consts_done = P.op("vector", lambda E, t=t_, v=v_: E.memset(t[:, :], v), s_v)

    sl_tgts = []
    red_tgts = []
    for k in range(16):
        if k < 8:
            src = wup_sl[:, TB * k:TB * (k + 1)]
        else:
            kk = k - 8
            src = wdn_sl[128 * (kk // 2):128 * (kk // 2 + 1),
                         TB * (kk % 2):TB * (kk % 2 + 1)]
        wts = [(s_v, red_tgts[k - 2])] if k >= 2 else []
        slot_sem = s_w0 if k % 2 == 0 else s_w1
        sl_tgts.append(dma("sync", wstage[:, k % 2, :], src, waits=wts, sem=slot_sem))
        red_tgts.append(P.op("vector", lambda E, k=k: E.tensor_reduce(
            acc16[:, k:k + 1], wstage[:, k % 2, :], AX.X, ALU.add,
            apply_absolute_value=True),
            s_v, waits=[(slot_sem, sl_tgts[k]), (s_v, consts_done)]))
    spacer("vector")
    r_up = P.op("vector", lambda E: E.tensor_reduce(al_pack[:, 0:1], acc16[:, 0:8],
                                                    AX.X, ALU.add), s_v,
                waits=[(s_v, red_tgts[-1])])
    alred_done = P.op("vector", lambda E: E.tensor_reduce(
        al_pack[:, 1:2], acc16[:, 8:16], AX.X, ALU.add), s_v,
        waits=[(s_v, r_up)])

    def emit_stats(sub, b, pr, with_rms, x_waits, extra_mm_waits=()):
        sl = slice(TB * b, TB * (b + 1))
        xbf = xbf2_sb if sub == 1 else xbf_sb
        cwts = list(x_waits)
        if ST.get("prev_sq_mm") is not None:
            cwts.append((s_t, ST["prev_sq_mm"]))
        cst = P.op("vector", lambda E, sl=sl, xbf=xbf: E.tensor_copy(
            xbf[:, :, :], x_sb[:, :, sl]), s_v, waits=cwts)
        mm = None
        for c in range(DC):
            mm = P.op("tensor", lambda E, c=c, xbf=xbf: E.matmul(
                pS1[0:1, :], ones_col_bf[:, :], xbf[:, c, :],
                start=(c == 0), stop=(c == DC - 1)),
                s_t if c == DC - 1 else None,
                waits=([(s_v, cst)] + list(extra_mm_waits)) if c == 0 else ())
        sum_mm = mm
        m_t = P.op("vector", lambda E: E.tensor_single_scalar(
            r0[:, :], pS1[0:1, :], 1.0 / D, ALU.mult), s_v, waits=[(s_t, sum_mm)])
        sq_t = P.op("scalar", lambda E, xbf=xbf: E.activation(
            xbf[:, :, :], xbf[:, :, :], AF.Square), s_s, waits=[(s_t, sum_mm)])
        mm = None
        for c in range(DC):
            mm = P.op("tensor", lambda E, c=c, xbf=xbf: E.matmul(
                pS2[0:1, :], ones_col_bf[:, :], xbf[:, c, :],
                start=(c == 0), stop=(c == DC - 1)),
                s_t if c == DC - 1 else None,
                waits=[(s_s, sq_t)] if c == 0 else ())
        sq_mm = mm
        ST["prev_sq_mm"] = sq_mm
        m2_t = P.op("vector", lambda E: E.tensor_tensor(
            r2[:, :], r0[:, :], r0[:, :], ALU.mult), s_v, waits=[(s_v, m_t)])
        var_t = P.op("vector", lambda E: E.scalar_tensor_tensor(
            r1[:, :], pS2[0:1, :], 1.0 / D, r2[:, :], ALU.mult, ALU.subtract), s_v,
            waits=[(s_t, sq_mm), (s_v, m2_t)])
        q_t = P.op("vector", lambda E: E.tensor_single_scalar(
            r2[:, :], r1[:, :], 1e-5, ALU.add), s_v, waits=[(s_v, var_t)])
        sq1 = P.op("scalar", lambda E: E.activation(r2[:, :], r2[:, :], AF.Sqrt),
                   s_s, waits=[(s_v, q_t)])
        ab_guard = [(s_t, ST["last_ab"])] if ST.get("last_ab") else []
        if with_rms:
            sq2 = P.op("scalar", lambda E: E.activation(r3[:, :], r1[:, :], AF.Sqrt),
                       s_s)
            dn_t = P.op("vector", lambda E: E.scalar_tensor_tensor(
                r1[:, :], r2[:, :], 1e-8, r3[:, :], ALU.mult, ALU.add), s_v,
                waits=[(s_s, sq2)])
            a_t = P.op("vector", lambda E: E.reciprocal(a_row[:, :], r1[:, :]),
                       s_v, waits=[(s_v, dn_t)] + ab_guard)
        else:
            a_t = P.op("vector", lambda E: E.reciprocal(a_row[:, :], r2[:, :]),
                       s_v, waits=[(s_s, sq1)] + ab_guard)
        b_t = P.op("vector", lambda E: E.scalar_tensor_tensor(
            b_row[:, :], r0[:, :], -1.0, a_row[:, :], ALU.mult, ALU.mult), s_v,
            waits=[(s_v, a_t)])
        return {"rows": b_t, "m": m_t, "sq_mm": sq_mm}

    def emit_bcast_ab(pr, rows_tgt, xn_done_tgt, half=False):
        lhs = half_row_f if half else ones_row_f
        wts = [(s_v, rows_tgt)]
        if xn_done_tgt:
            wts.append((s_v, xn_done_tgt))
        P.op("tensor", lambda E, lhs=lhs: E.matmul(
            pBA[:, :], lhs[:, :], a_row[0:1, :], start=True, stop=True),
            s_t, waits=wts)
        tb = P.op("tensor", lambda E, lhs=lhs: E.matmul(
            pBB[:, :], lhs[:, :], b_row[0:1, :], start=True, stop=True), s_t)
        ST["last_ab"] = tb
        return tb

    def emit_xn(sub, b, bcast_tgt, x_waits):
        sl = slice(TB * b, TB * (b + 1))
        dst = xn2_sb if sub == 1 else xn_sb
        tmp = oflat if sub == 1 else t1_sb
        t = None
        for c in range(DC):
            t1t = P.op("vector", lambda E, c=c, sl=sl, tmp=tmp: E.tensor_tensor(
                tmp[:, :], x_sb[:, c, sl], pBA[:, :], ALU.mult), s_v,
                waits=([(s_t, bcast_tgt)] + list(x_waits)) if c == 0 else ())
            t = P.op("vector", lambda E, c=c, dst=dst, tmp=tmp: E.tensor_tensor(
                dst[:, c, :], tmp[:, :], pBB[:, :], ALU.add), s_v,
                waits=[(s_v, t1t)])
        return t

    pr = 0
    st0 = emit_stats(0, 0, pr, True, x_waits=[(s_x, x_tgt)],
                     extra_mm_waits=[(s_v, consts_done)])
    bc0 = emit_bcast_ab(pr, st0["rows"], None, half=True)
    last_xn = emit_xn(0, 0, bc0, x_waits=[(s_x, x_tgt)])

    g1 = dma("gpsimd", al_in[:, :], al_pack[:, :], waits=[(s_v, alred_done)])
    cc1 = P.op("gpsimd", lambda E: E.collective_compute(
        "AllReduce", ALU.add, replica_groups=[list(range(NCORES))],
        ins=[al_in.ap().opt()], outs=[al_out.ap().opt()]),
        s_cc, 1, waits=[(s_gq, g1)])
    g2 = dma("gpsimd", al_gath[:, :], al_out[:, :], waits=[(s_cc, cc1)])
    mm1 = P.op("tensor", lambda E: E.matmul(pS1[0:1, 0:2], sc_col[:, :],
                                            al_gath[:, :], start=True, stop=True),
               s_t, waits=[(s_gq, g2), (s_v, last_xn)])
    cp1 = P.op("scalar", lambda E: E.copy(th_src[:, :], pS1[0:1, 0:2]), s_s,
               waits=[(s_t, mm1)])
    mm2 = P.op("tensor", lambda E: E.matmul(pS2[:, 0:2], ones_row_f[:, :],
                                            th_src[0:1, :], start=True, stop=True),
               s_t, waits=[(s_s, cp1)])
    th_done = None
    for dst_, col_, sc_ in ((th_up_p, 0, 1.0), (th_up_n, 0, -1.0),
                            (th_dn_p, 1, 1.0), (th_dn_n, 1, -1.0)):
        th_done = P.op("vector", lambda E, d=dst_, c=col_, s=sc_: E.tensor_single_scalar(
            d[:, :], pS2[:, c:c + 1], s, ALU.mult), s_v, waits=[(s_t, mm2)])

    last_comb = {}
    slot_last = {}
    WSEMS = (s_w0, s_w1, s_w2, s_w3)
    PAIRS = ((tneg0, tpos0), (tneg1, tpos1), (tneg2, tpos2))
    tern_up_v = tern_up_g = None
    for m in range(128):
        up = m < 64
        mm_ = m if up else m - 64
        j = mm_ // 2
        half = mm_ % 2
        if up:
            src = wupT[128 * (j // 4):128 * (j // 4 + 1),
                       KIB * (j % 4) + TB * half:KIB * (j % 4) + TB * (half + 1)]
            dst8 = wup8
            slot = m % 4
            stage = wstage4
        else:
            src = wdnT[128 * j:128 * (j + 1), TB * half:TB * (half + 1)]
            dst8 = wdn8
            slot = 2 + (m % 2)
            stage = wstage4
        slot_sem = WSEMS[slot]
        wts = []
        prev = slot_last.get(slot)
        if prev is not None:
            wts.append(prev)
        if m < 4:
            wts.append((s_v, alred_done))
        ld = dma("sync", stage[:, slot, :], src, waits=wts, sem=slot_sem)
        eng = "gpsimd" if (m % 4 == 3) else "vector"
        csem = CSEM[eng]
        thp = th_up_p if up else th_dn_p
        thn = th_up_n if up else th_dn_n
        dst_sl = (dst8, j, TB * half, TB * (half + 1))
        tn, tp = PAIRS[m % 3]
        swts = [(slot_sem, ld), (s_v, th_done)]
        if last_comb.get(m % 3) is not None:
            swts.append(last_comb[m % 3])
        P.op("scalar", lambda E, slot=slot, thn=thn, tn=tn, stage=stage: E.activation(
            tn[:, :], stage[:, slot, :], AF.Sign, bias=thn[:, 0:1]),
            waits=swts)
        sg = P.op("scalar", lambda E, slot=slot, thp=thp, tp=tp, stage=stage:
                  E.activation(tp[:, :], stage[:, slot, :], AF.Sign,
                               bias=thp[:, 0:1]), s_s)
        tt = P.op(eng, lambda E, d=dst_sl, tn=tn, tp=tp: E.tensor_tensor(
            d[0][:, d[1], d[2]:d[3]], tn[:, :], tp[:, :], ALU.add), csem,
            waits=[(s_s, sg)])
        last_comb[m % 3] = (csem, tt)
        slot_last[slot] = (csem, tt)
        if up and mm_ == 63:
            tern_up_v = P.total(s_v)
            tern_up_g = P.total(s_g)
            tern_up_s = P.total(s_s)
    tern_all_v = P.total(s_v)
    tern_all_g = P.total(s_g)
    tern_all_s = P.total(s_s)

    ep_tgt = [[None] * DC for _ in range(3 * NB)]

    def emit_ffn_block(sub, b, gb, xn_tgt, next_stats, next_xn=None):
        sl = slice(TB * b, TB * (b + 1))
        gelu_t = {}
        sq_t = {}
        hh_mm = {}
        upstop = {}
        cast_t = {}
        CAST_CYCLE = ("vector", "scalar", "vector", "scalar", "gpsimd")
        for fc in range(FC):
            eng = CAST_CYCLE[fc % 5]
            csem = CSEM[eng]
            wts = []
            if upstop.get(fc - 2) is not None:
                wts.append((s_t, upstop[fc - 2]))
            if sub == 0:
                wts += [(s_v, tern_up_v), (s_g, tern_up_g), (s_s, tern_up_s)]
            else:
                wts.append((s_sp, ST["w8_reload_up"]))
            if eng == "scalar":
                cast_t[fc] = (csem, P.op(eng, lambda E, fc=fc: E.activation(
                    castup[:, fc % 2, :, :],
                    wup8[:, (fc // 8)::4, 128 * (fc % 8):128 * (fc % 8 + 1)],
                    AF.Identity), csem, waits=wts))
            else:
                cast_t[fc] = (csem, P.op(eng, lambda E, fc=fc: E.tensor_copy(
                    castup[:, fc % 2, :, :],
                    wup8[:, (fc // 8)::4, 128 * (fc % 8):128 * (fc % 8 + 1)]),
                    csem, waits=wts))
            w0 = [cast_t[fc], (s_v, xn_tgt)]
            if gelu_t.get(fc - 2) is not None:
                w0.append((s_s, gelu_t[fc - 2]))
            mm = None
            for dc in range(DC):
                mm = P.op("tensor", lambda E, fc=fc, dc=dc: E.matmul(
                    pUP[fc % 2][:, :], castup[:, fc % 2, dc, :], xn_sb[:, dc, :],
                    start=(dc == 0), stop=(dc == DC - 1)),
                    s_t if dc == DC - 1 else None, waits=w0 if dc == 0 else ())
            upstop[fc] = mm
            gelu_t[fc] = P.op("scalar", lambda E, fc=fc: E.activation(
                h_sb[:, fc, :], pUP[fc % 2][:, :], AF.Gelu), s_s,
                waits=[(s_t, upstop[fc])])
            if fc >= 2:
                f2 = fc - 2
                hh_mm[f2] = P.op("tensor", lambda E, f2=f2: E.matmul(
                    pS1[0:1, :], ones_col_bf[:, :], hsq_sb[:, f2 % 2, :],
                    start=(f2 == 0), stop=(f2 == FC - 1)),
                    s_t, waits=[(s_s, sq_t[f2])])
            sqw = [(s_s, gelu_t[fc])]
            if hh_mm.get(fc - 2) is not None:
                sqw.append((s_t, hh_mm[fc - 2]))
            if sub == 0 and b == 0 and fc < 2:
                sqw += [(s_v, tern_all_v), (s_g, tern_all_g)]
            sq_t[fc] = P.op("scalar", lambda E, fc=fc: E.activation(
                hsq_sb[:, fc % 2, :], h_sb[:, fc, :], AF.Square), s_s, waits=sqw)
        for f2 in (FC - 2, FC - 1):
            hh_mm[f2] = P.op("tensor", lambda E, f2=f2: E.matmul(
                pS1[0:1, :], ones_col_bf[:, :], hsq_sb[:, f2 % 2, :],
                start=(f2 == 0), stop=(f2 == FC - 1)),
                s_t, waits=[(s_s, sq_t[f2])])
        hh_done = hh_mm[FC - 1]
        cpre = P.op("vector", lambda E: E.tensor_single_scalar(
            r3[:, :], pS1[0:1, :], 1.0 / DF, ALU.mult), s_v,
            waits=[(s_t, hh_done)])
        csq = P.op("scalar", lambda E: E.activation(r3[:, :], r3[:, :], AF.Sqrt),
                   s_s, waits=[(s_v, cpre)])
        ct1 = P.op("vector", lambda E: E.tensor_single_scalar(
            r3[:, :], r3[:, :], 1e-8, ALU.add), s_v, waits=[(s_s, csq)])
        cb_guard = [(s_t, ST["last_cb"])] if ST.get("last_cb") else []
        ct2 = P.op("vector", lambda E: E.reciprocal(c_row[:, :], r3[:, :]), s_v,
                   waits=cb_guard + [(s_v, ct1)])
        st_info = None
        if next_stats is not None:
            st_info = next_stats(extra_mm_waits=[(s_s, csq)])
        cwts = [(s_v, ct2)]
        if st_info is not None:
            cwts.append((s_v, st_info["m"]))
        cb = P.op("tensor", lambda E: E.matmul(pS1[:, :], quar_row_f[:, :],
                                               c_row[0:1, :], start=True, stop=True),
                  s_t, waits=cwts)
        ST["last_cb"] = cb
        cwts2 = [(s_t, cb)]
        if sub == 0 and b == 0:
            cwts2 += [(s_v, tern_all_v), (s_g, tern_all_g)]
        csb = P.op("scalar", lambda E: E.copy(csb_sb[:, :], pS1[:, :]), s_s,
                   waits=cwts2)
        dnstop = {}
        castd_t = {}
        qend = {}
        ep_list = []
        nxn_result = [None]
        CAST_CYCLE_D = ("scalar", "vector", "gpsimd", "vector", "scalar")

        def emit_castd(dc, q):
            key = 4 * dc + q
            eng = CAST_CYCLE_D[key % 5]
            csem = CSEM[eng]
            wts = []
            if qend.get(key - 2) is not None:
                wts.append((s_t, qend[key - 2]))
            if sub == 0 and b == 0 and key < 2:
                wts += [(s_v, tern_all_v), (s_g, tern_all_g), (s_s, tern_all_s)]
            if sub == 2 and key < 2:
                wts.append((s_sp, ST["w8_reload_dn"]))
            if eng == "scalar":
                castd_t[key] = (csem, P.op(eng, lambda E, q=q, dc=dc, key=key:
                                           E.activation(
                    castdn[:, key % 2, :, :],
                    wdn8[:, 8 * q:8 * (q + 1), 128 * dc:128 * (dc + 1)],
                    AF.Identity), csem, waits=wts))
            else:
                castd_t[key] = (csem, P.op(eng, lambda E, q=q, dc=dc, key=key:
                                           E.tensor_copy(
                    castdn[:, key % 2, :, :],
                    wdn8[:, 8 * q:8 * (q + 1), 128 * dc:128 * (dc + 1)]),
                    csem, waits=wts))

        for dc in range(DC):
            emit_castd(dc, 0)
            emit_castd(dc, 1)
            if dc == 2 and next_xn is not None and st_info is not None:
                nxn_result[0] = next_xn(st_info)
            w0 = [(s_s, gelu_t[FC - 1])]
            if dc >= 2 and ep_tgt[gb][dc - 2] is not None:
                w0.append((s_v, ep_tgt[gb][dc - 2]))
            mm = None
            for fc in range(FC):
                if fc == 8:
                    emit_castd(dc, 2)
                if fc == 16:
                    emit_castd(dc, 3)
                fw = list(w0) if fc == 0 else []
                if fc % 8 == 0:
                    fw.append(castd_t[4 * dc + fc // 8])
                mm = P.op("tensor", lambda E, dc=dc, fc=fc: E.matmul(
                    pDN[dc % 2][:, :], castdn[:, (4 * dc + fc // 8) % 2, fc % 8, :],
                    h_sb[:, fc, :], start=(fc == 0), stop=(fc == FC - 1)),
                    s_t if fc % 8 == 7 else None, waits=fw)
                if fc % 8 == 7:
                    qend[4 * dc + fc // 8] = mm
            dnstop[dc] = mm
            e1 = P.op("vector", lambda E, dc=dc: E.tensor_tensor(
                t1_sb[:, :], pDN[dc % 2][:, :], csb_sb[:, :], ALU.mult), s_v,
                waits=[(s_t, dnstop[dc]), (s_s, csb)])
            ep = P.op("vector", lambda E, dc=dc, sl=sl: E.tensor_tensor(
                x_sb[:, dc, sl], x_sb[:, dc, sl], t1_sb[:, :], ALU.add), s_v,
                waits=[(s_v, e1)])
            ep_tgt[gb][dc] = ep
            ep_list.append(ep)
        return ep_list, st_info, nxn_result[0]

    def emit_ssm_block(b, gb, xn_tgt, next_stats, prev_guard):
        sl = slice(TB * b, TB * (b + 1))
        pc = b % 2
        cpc_o, cpc_n = cpc_pp[pc], cpc_pp[1 - pc]
        stc_o, stc_n = stc_pp[pc], stc_pp[1 - pc]
        evd = {}
        evi = {}
        decstop = {}
        injstop = {}
        for ec in range(DC):
            wts = [(s_gq, ST["wdec_ld"])] if ec == 0 else []
            if ec == 0:
                wts.append((s_v, xn_tgt))
            if evd.get(ec - 2) is not None:
                wts.append((s_s, evd[ec - 2]))
            mm = None
            for dc in range(DC):
                mm = P.op("tensor", lambda E, ec=ec, dc=dc: E.matmul(
                    pUP[ec % 2][:, :], wdec_sb[:, dc, 128 * ec:128 * (ec + 1)],
                    xn2_sb[:, dc, :], start=(dc == 0), stop=(dc == DC - 1)),
                    s_t if dc == DC - 1 else None, waits=wts if dc == 0 else ())
            decstop[ec] = mm
            ewts = [(s_t, decstop[ec]), (s_aq, ST["bias_ld"])]
            if prev_guard is not None:
                ewts.append((s_v, prev_guard["dec"][ec]))
            evd[ec] = P.op("scalar", lambda E, ec=ec: E.activation(
                dec_sb[:, ec, :], pUP[ec % 2][:, :], AF.Sigmoid,
                bias=bdec_sb[:, ec:ec + 1]), s_s, waits=ewts)
        for ec in range(DC):
            wts = [(s_gq, ST["winp_ld"])] if ec == 0 else []
            if evi.get(ec - 2) is not None:
                wts.append((s_s, evi[ec - 2]))
            mm = None
            for dc in range(DC):
                mm = P.op("tensor", lambda E, ec=ec, dc=dc: E.matmul(
                    pDN[ec % 2][:, :], winp_sb[:, dc, 128 * ec:128 * (ec + 1)],
                    xn2_sb[:, dc, :], start=(dc == 0), stop=(dc == DC - 1)),
                    s_t if dc == DC - 1 else None, waits=wts if dc == 0 else ())
            injstop[ec] = mm
            ewts = [(s_t, injstop[ec]), (s_aq, ST["bias_ld"])]
            if prev_guard is not None:
                ewts.append((s_v, prev_guard["inj"][ec]))
            evi[ec] = P.op("scalar", lambda E, ec=ec: E.activation(
                inj_sb[:, ec, :], pDN[ec % 2][:, :], AF.Identity,
                bias=binp_sb[:, ec:ec + 1]), s_s, waits=ewts)
        cp_t = {}
        for ec in range(DC):
            cp_t[ec] = P.op("vector", lambda E, ec=ec: E.tensor_tensor_scan(
                cp32_sb[:, ec, :], dec_sb[:, ec, :], zeros_sb[:, :], 1.0,
                ALU.mult, ALU.add), s_v, waits=[(s_s, evd[ec])])
        cpf_w = list(ST["cpf_guard"]) if b == 0 else []
        cpf = None
        for ec in range(DC):
            cpf = P.op("vector", lambda E, ec=ec, sl=sl: E.tensor_single_scalar(
                cp_full[:, ec, sl], cp32_sb[:, ec, :], cpc_o[:, ec:ec + 1],
                ALU.mult), s_v,
                waits=(cpf_w if ec == 0 else []) + [(s_v, cp_t[ec])])
        cpn = P.op("vector", lambda E: E.tensor_tensor(
            cpc_n[:, :], cp32_sb[:, :, TB - 1:TB], cpc_o[:, :], ALU.mult), s_v,
            waits=[(s_v, cpf)])
        spacer("vector")
        dec_guard = {}
        inj_guard = {}
        ep = None
        for ec in range(DC):
            inj_guard[ec] = P.op("vector", lambda E, ec=ec: E.tensor_copy(
                iflat[:, :], inj_sb[:, ec, :]), s_v, waits=[(s_s, evi[ec])])
            dec_guard[ec] = P.op("vector", lambda E, ec=ec: E.tensor_tensor_scan(
                oflat[:, :], dec_sb[:, ec, :], iflat[:, :], 0.0,
                ALU.mult, ALU.add), s_v, waits=[(s_v, inj_guard[ec])])
            sfx = P.op("vector", lambda E, ec=ec: E.scalar_tensor_tensor(
                iflat[:, :], cp32_sb[:, ec, :], stc_o[:, ec:ec + 1], oflat[:, :],
                ALU.mult, ALU.add), s_v, waits=[(s_v, dec_guard[ec])])
            ep = P.op("vector", lambda E, ec=ec, sl=sl: E.tensor_tensor(
                x_sb[:, ec, sl], x_sb[:, ec, sl], iflat[:, :], ALU.add), s_v,
                waits=[(s_v, sfx)])
            P.op("vector", lambda E, ec=ec: E.tensor_copy(
                stc_n[:, ec:ec + 1], iflat[:, TB - 1:TB]), s_v, waits=[(s_v, ep)])
        st_info = next_stats(extra_mm_waits=()) if next_stats is not None else None
        for dc in range(DC):
            ep_tgt[gb][dc] = ep
        return {"ep": ep, "cpf": cpf, "cpn": cpn, "st_info": st_info,
                "guard": {"dec": dec_guard, "inj": inj_guard},
                "carry_new": 1 - pc}

    NBF = debug_nb if sublayers == 1 else NB
    for sub in [0, 1, 2][:sublayers]:
        if sub in (0, 2):
            for b in range(NBF):
                gb = (0 if sub == 0 else 2 * NB) + b
                npr = 1 - pr
                nxt = None
                if b < NBF - 1:
                    if sub == 0:
                        nxt = (lambda extra_mm_waits, b=b, npr=npr: emit_stats(
                            0, b + 1, npr, True, x_waits=[(s_x, x_tgt)],
                            extra_mm_waits=extra_mm_waits))
                    else:
                        nxt = (lambda extra_mm_waits, b=b, npr=npr: emit_stats(
                            2, b + 1, npr, True, x_waits=[(s_v, ST["fix_done"])],
                            extra_mm_waits=extra_mm_waits))

                nsub = sub if b < NB - 1 else 1
                nb = b + 1 if b < NB - 1 else 0
                prev_xn = last_xn

                def mk_next_xn(nsub=nsub, nb=nb, npr=npr, prev_xn=prev_xn, sub=sub):
                    def f(sti):
                        bc = emit_bcast_ab(npr, sti["rows"], prev_xn,
                                           half=(nsub != 1))
                        xw = [(s_v, ST["fix_done"])] if sub == 2 else []
                        return emit_xn(nsub, nb, bc, x_waits=xw)
                    return f

                eps, st_info, nxn = emit_ffn_block(
                    sub, b, gb, last_xn, nxt, mk_next_xn() if nxt is not None else None)
                if nxn is not None:
                    last_xn = nxn
                    pr = npr
                if sub == 2:
                    yr = yT.rearrange("(c p) t -> p c t", p=128)
                    sl = slice(TB * b, TB * (b + 1))
                    ST["store"] = dma("scalar", yr[:, :, sl], x_sb[:, :, sl],
                                      waits=[(s_v, eps[-1])])
        else:
            spill_up = dma("sync", sp_up[:, :, :], wup8[:, :, :],
                           waits=[(s_v, tern_all_v), (s_g, tern_all_g)], sem=s_sp)
            wdec_ld = dma("gpsimd", wdec_sb[:, :, :],
                          wdecT.rearrange("(c p) e -> p c e", p=128),
                          waits=[(s_sp, spill_up), (s_v, P.total(s_v)),
                                 (s_g, P.total(s_g))])
            winp_ld = dma("gpsimd", winp_sb[:, :, :],
                          winpT.rearrange("(c p) e -> p c e", p=128))
            ST["wdec_ld"] = winp_ld
            ST["winp_ld"] = winp_ld
            spill_dn = dma("sync", sp_dn[:, :, :], wdn8[:, :, :],
                           waits=[(s_sp, spill_up), (s_v, P.total(s_v)),
                                  (s_g, P.total(s_g))], sem=s_sp)
            ST["cpf_guard"] = [(s_sp, spill_dn)]
            npr = 1 - pr
            st1 = emit_stats(1, 0, npr, False,
                             x_waits=[(s_v, ep_tgt[NB - 1][DC - 1])])
            bc1 = emit_bcast_ab(npr, st1["rows"], last_xn, half=False)
            last_xn = emit_xn(1, 0, bc1, x_waits=[(s_v, ep_tgt[NB - 1][DC - 1])])
            pr = npr
            res = None
            prev_guard = None
            for b in range(NB):
                gb = NB + b
                npr = 1 - pr
                nxt = None
                if b < NB - 1:
                    nxt = (lambda extra_mm_waits, b=b, npr=npr: emit_stats(
                        1, b + 1, npr, False,
                        x_waits=[(s_v, ep_tgt[b + 1][DC - 1])],
                        extra_mm_waits=extra_mm_waits))
                res = emit_ssm_block(b, gb, last_xn, nxt, prev_guard)
                prev_guard = res["guard"]
                if res["st_info"] is not None:
                    bc = emit_bcast_ab(npr, res["st_info"]["rows"], last_xn, half=False)
                    last_xn = emit_xn(1, b + 1, bc, x_waits=[(s_v, res["ep"])])
                    pr = npr
            cnew = res["carry_new"]
            pk1 = P.op("vector", lambda E: E.tensor_copy(
                exch_sb[:, 0:8], cpc_pp[cnew][:, :]), s_v, waits=[(s_v, res["cpn"])])
            pk2 = P.op("vector", lambda E: E.tensor_copy(
                exch_sb[:, 8:16], stc_pp[cnew][:, :]), s_v, waits=[(s_v, res["ep"])])
            ge = dma("gpsimd", ag_in[:, :], exch_sb[:, :], waits=[(s_v, pk2)])
            cct = P.op("gpsimd", lambda E: E.collective_compute(
                "AllGather", ALU.bypass, replica_groups=[list(range(NCORES))],
                ins=[ag_in.ap().opt()], outs=[ag_out.ap().opt()]),
                s_cc, 1, waits=[(s_gq, ge)])
            gg = dma("gpsimd", gath_sb[:, :, :],
                     ag_out.ap().rearrange("j p k -> p j k"), waits=[(s_cc, cct)])
            pt = P.op("vector", lambda E: E.tensor_copy(S_pp[0][:, :],
                                                        gath_sb[:, 0, 8:16]), s_v,
                      waits=[(s_gq, gg), (s_aq, misc_tgt)])
            spacer("vector")
            P.op("vector", lambda E: E.tensor_copy(cs_sb[:, :, 0], S_pp[0][:, :]),
                 s_v, waits=[(s_v, pt)])
            sj = 0
            for j in range(1, 8):
                spacer("vector")
                pt = P.op("vector", lambda E, j=j, sj=sj: E.tensor_tensor(
                    tmpS[:, :], gath_sb[:, j, 0:8], S_pp[sj][:, :], ALU.mult), s_v,
                    waits=[(s_v, pt)])
                spacer("vector")
                pt = P.op("vector", lambda E, j=j, sj=sj: E.tensor_tensor(
                    S_pp[1 - sj][:, :], tmpS[:, :], gath_sb[:, j, 8:16], ALU.add),
                    s_v, waits=[(s_v, pt)])
                sj = 1 - sj
                spacer("vector")
                pt = P.op("vector", lambda E, j=j, sj=sj: E.tensor_copy(
                    cs_sb[:, :, j], S_pp[sj][:, :]), s_v, waits=[(s_v, pt)])
            spacer("vector")
            ml = P.op("vector", lambda E: E.tensor_tensor(
                tmp88_sb[:, :, :], cs_sb[:, :, :], selm_sb[:, :, :], ALU.mult), s_v,
                waits=[(s_v, pt)])
            spacer("vector")
            rd = None
            for c in range(DC):
                rd = P.op("vector", lambda E, c=c: E.tensor_reduce(
                    sin_sb[:, c:c + 1], tmp88_sb[:, c, :], AX.X, ALU.add), s_v,
                    waits=[(s_v, ml)] if c == 0 else ())
            spacer("vector")
            fx = None
            for c in range(DC):
                fx = P.op("vector", lambda E, c=c: E.scalar_tensor_tensor(
                    x_sb[:, c, :], cp_full[:, c, :], sin_sb[:, c:c + 1],
                    x_sb[:, c, :], ALU.mult, ALU.add), s_v,
                    waits=[(s_v, rd), (s_v, res["cpf"])] if c == 0 else ())
            ST["fix_done"] = fx
            if sublayers > 2:
                rl_up = dma("sync", wup8[:, :, :], sp_up[:, :, :],
                            waits=[(s_t, P.total(s_t)), (s_gq, P.total(s_gq)),
                                   (s_sp, spill_dn)], sem=s_sp)
                rl_dn = dma("sync", wdn8[:, :, :], sp_dn[:, :, :],
                            waits=[(s_v, fx), (s_sp, rl_up)], sem=s_sp)
                ST["w8_reload_up"] = rl_up
                ST["w8_reload_dn"] = rl_dn
                npr = 1 - pr
                st2 = emit_stats(2, 0, npr, True, x_waits=[(s_v, fx)])
                bc2 = emit_bcast_ab(npr, st2["rows"], last_xn, half=True)
                last_xn = emit_xn(2, 0, bc2, x_waits=[(s_v, fx)])
                pr = npr

    if sublayers < 3:
        yr = yT.rearrange("(c p) t -> p c t", p=128)
        dma("scalar", yr[:, :, :], x_sb[:, :, :],
            waits=[(s_v, P.total(s_v)), (s_s, P.total(s_s)), (s_g, P.total(s_g))])
    if sublayers == 2:
        for nm, t_, shp in (
            ("s_dec", dec_sb, [128, DC, TB]), ("s_inj", inj_sb, [128, DC, TB]),
            ("s_cp32", cp32_sb, [128, DC, TB]), ("s_sin", sin_sb, [128, DC]),
            ("s_gath", gath_sb, [128, 8, 16]), ("s_exch", exch_sb, [128, 16]),
            ("s_cpc", cpc_pp[0], [128, DC]), ("s_stc", stc_pp[0], [128, DC])):
            dr = nc.dram_tensor(nm, shp, F32, kind="ExternalOutput").ap()
            dma("scalar", dr, t_[tuple(slice(None) for _ in shp)],
                waits=[(s_v, P.total(s_v)), (s_s, P.total(s_s)),
                       (s_g, P.total(s_g)), (s_t, P.total(s_t))])
        drc = nc.dram_tensor("s_cpf", [128, DC, TC], BF16, kind="ExternalOutput").ap()
        dma("scalar", drc, cp_full[:, :, :],
            waits=[(s_v, P.total(s_v))])
        drx = nc.dram_tensor("s_xn2", [128, DC, TB], BF16, kind="ExternalOutput").ap()
        dma("scalar", drx, xn2_sb[:, :, :], waits=[(s_v, P.total(s_v))])
    if sublayers == 1 and debug_nb == 1:
        dbg = {}
        for nm, t_, shp, dt_ in (
            ("d_wup8", wup8, [128, FC, KIB], F8), ("d_wdn8", wdn8, [128, FC, KIB], F8),
            ("d_xn", xn_sb, [128, DC, TB], BF16), ("d_h", h_sb, [128, FC, TB], BF16),
            ("d_csb", csb_sb, [128, TB], F32), ("d_arow", a_row, [1, TB], F32),
            ("d_brow", b_row, [1, TB], F32), ("d_crow", c_row, [1, TB], F32),
            ("d_thup", th_up_p, [128, 1], F32), ("d_thdn", th_dn_p, [128, 1], F32),
            ("d_alg", al_gath, [128, 2], F32), ("d_alp", al_pack, [128, 2], F32)):
            dr = nc.dram_tensor(nm, shp, dt_, kind="ExternalOutput").ap()
            dma("scalar", dr, t_[tuple(slice(None) for _ in shp)],
                waits=[(s_v, P.total(s_v)), (s_s, P.total(s_s)),
                       (s_g, P.total(s_g)), (s_t, P.total(s_t))])

    P.op("sync", lambda E: E.nop(), waits=[(s_aq, P.total(s_aq)),
                                           (s_dq, P.total(s_dq)),
                                           (s_gq, P.total(s_gq)),
                                           (s_x, P.total(s_x)),
                                           (s_w0, P.total(s_w0)),
                                           (s_w1, P.total(s_w1)),
                                           (s_w2, P.total(s_w2)),
                                           (s_w3, P.total(s_w3)),
                                           (s_sp, P.total(s_sp))])

    block = es.enter_context(nc.Block())

    def runner(oplist):
        def f(E):
            for fn in oplist:
                fn(E)
        return f

    block.tensor(runner(P.ops["tensor"]))
    block.scalar(runner(P.ops["scalar"]))
    block.vector(runner(P.ops["vector"]))
    block.gpsimd(runner(P.ops["gpsimd"]))
    block.sync(runner(P.ops["sync"]))
    es.close()
    return nc


def kernel(L0_attn_x_seq, L0_attn_W_decay, L0_attn_b_decay, L0_attn_W_input,
           L0_attn_b_input, L0_ffn_w_up, L0_ffn_w_down, sublayers=3, debug_nb=NB,
           _dbg_out=None):
    x = np.asarray(L0_attn_x_seq)[0]
    xT = np.ascontiguousarray(x.T.astype(np.float32))
    wupT = np.ascontiguousarray(np.asarray(L0_ffn_w_up).T.astype(np.float32))
    wdnT = np.ascontiguousarray(np.asarray(L0_ffn_w_down).T.astype(np.float32))
    wdecT = np.ascontiguousarray(np.asarray(L0_attn_W_decay).T.astype(np.float32))
    winpT = np.ascontiguousarray(np.asarray(L0_attn_W_input).T.astype(np.float32))
    bdec = np.ascontiguousarray(
        np.asarray(L0_attn_b_decay).reshape(DC, 128).T.astype(np.float32))
    binp = np.ascontiguousarray(
        np.asarray(L0_attn_b_input).reshape(DC, 128).T.astype(np.float32))

    nc = build_nc(sublayers=sublayers, debug_nb=debug_nb)
    in_maps = []
    for i in range(NCORES):
        sel = np.zeros((128, DC, 8), np.float32)
        if i > 0:
            sel[:, :, i - 1] = 1.0
        in_maps.append({
            "xT": np.ascontiguousarray(xT[:, i * TC:(i + 1) * TC]),
            "wupT": wupT, "wdnT": wdnT, "wdecT": wdecT, "winpT": winpT,
            "wup_sl": np.ascontiguousarray(wupT[128 * i:128 * (i + 1), :]),
            "wdn_sl": np.ascontiguousarray(wdnT[512 * i:512 * (i + 1), :]),
            "bdec": bdec, "binp": binp, "selm": sel,
        })
    res = run_bass_kernel_spmd(nc, in_maps, core_ids=list(range(NCORES)))
    if _dbg_out is not None:
        for r in res.results:
            _dbg_out.append({k_: np.asarray(v_) for k_, v_ in r.items()})
    yT_full = np.concatenate([r["yT"] for r in res.results], axis=1)
    return np.ascontiguousarray(yT_full.T)[None].astype(np.float32)
